# version 02 (final)
# baseline (speedup 1.0000x reference)
import numpy as np
import ml_dtypes
import concourse.bass as bass
import concourse.mybir as mybir
from concourse.bass_utils import run_bass_kernel_spmd
from concourse.bass import IndirectOffsetOnAxis

F32 = mybir.dt.float32
BF16 = mybir.dt.bfloat16
AF = mybir.ActivationFunctionType
ALU = mybir.AluOpType
AX = mybir.AxisListType
BF = ml_dtypes.bfloat16

NCORES = 8
B, S, D = 2, 8192, 1024
TLOC = 2048
DIN = 2048
NST = 16
DTR = 64
NLAYER = 8
CSL = DIN // NCORES
EPS = 1e-5

CORE_IDS = list(range(NCORES))
ENGS = ("sync", "tensor", "vector", "scalar", "gpsimd")


class P:

    def __init__(self):
        self.items = {e: [] for e in ENGS}
        self.count = {e: 0 for e in ENGS}
        self.last_tok = {e: None for e in ENGS}

    def add(self, eng, builder, deps=(), dma=False, inc=True, serial=True):
        flat = []
        for d in deps:
            if d is None:
                continue
            if isinstance(d, tuple) and d and isinstance(d[0], tuple):
                flat.extend(x for x in d if x is not None)
            else:
                flat.append(d)
        deps = flat
        for d in deps:
            assert d[1] is not None, "dep on un-inc'd instruction"
        if serial and eng != "tensor" and self.last_tok[eng] is not None:
            deps.append(self.last_tok[eng])
        step = 16 if dma else 1
        if inc:
            self.count[eng] += step
        tok = (eng, self.count[eng] if inc else None)
        self.items[eng].append((builder, deps, inc, step))
        if inc:
            self.last_tok[eng] = tok
        return tok

    def emit(self, nc, block, sems):
        def mk(eng_name):
            def run(engobj):
                watermark = {e: 0 for e in ENGS}
                for (builder, deps, inc, step) in self.items[eng_name]:
                    for (de, dv) in deps:
                        if dv > watermark[de]:
                            engobj.wait_ge(sems[de], dv)
                            watermark[de] = dv
                    inst = builder(engobj)
                    if inc:
                        assert inst is not None, f"{eng_name} item returned None"
                        inst.then_inc(sems[eng_name], step)
            return run

        for e, attr in (("sync", "sync"), ("tensor", "tensor"),
                        ("vector", "vector"), ("scalar", "scalar"),
                        ("gpsimd", "gpsimd")):
            if self.items[e]:
                getattr(block, attr)(mk(e))

    def dma(self, eng, out, in_, deps=()):
        return self.add(eng, lambda E: E.dma_start(out=out, in_=in_),
                        deps=deps, dma=True)

    def op(self, eng, name, *a, deps=(), **kw):
        return self.add(eng, lambda E: getattr(E, name)(*a, **kw), deps=deps)

    def mm(self, out, lhsT, rhs, start, stop, deps=(), inc=False, **kw):
        return self.add("tensor",
                        lambda E: E.matmul(out, lhsT, rhs, start=start,
                                           stop=stop, **kw),
                        deps=deps, inc=inc, serial=False)


def build_program(fill):
    import contextlib
    nc = bass.Bass("TRN2", target_bir_lowering=False)
    with contextlib.ExitStack() as stack:
        p = P()
        fill(nc, p, stack)
        sems = {e: stack.enter_context(nc.semaphore("sem_" + e)) for e in ENGS}
        with nc.Block() as block:
            p.emit(nc, block, sems)
    return nc


def _run(nc, in_maps):
    res = run_bass_kernel_spmd(nc, in_maps, core_ids=CORE_IDS)
    return res.results


def cdiv(a, b):
    return (a + b - 1) // b


def sb(nc, stack, name, shape, dtype):
    return stack.enter_context(nc.sbuf_tensor(name, shape, dtype))


def ps(nc, stack, name, shape):
    return stack.enter_context(nc.psum_tensor(name, shape, F32))


_PREP_COLS = 8192


def build_prep(nelem):
    assert nelem % (128 * _PREP_COLS) == 0
    nt = nelem // (128 * _PREP_COLS)

    def fill(nc, p, stack):
        x = nc.dram_tensor("x", [nt * 128, _PREP_COLS], F32, kind="ExternalInput").ap()
        y = nc.dram_tensor("y", [nt * 128, _PREP_COLS], BF16, kind="ExternalOutput").ap()
        xr = x.rearrange("(n p) c -> n p c", p=128)
        yr = y.rearrange("(n p) c -> n p c", p=128)
        xf = sb(nc, stack, "xf", [128, 2 * _PREP_COLS], F32)
        xb = sb(nc, stack, "xb", [128, 2 * _PREP_COLS], BF16)
        st_tok = [None, None]
        cp_tok = [None, None]
        for i in range(nt):
            pp = i % 2
            sl = slice(pp * _PREP_COLS, (pp + 1) * _PREP_COLS)
            ld = p.dma("sync", xf[:, sl], xr[i], deps=(cp_tok[pp],))
            cp_tok[pp] = p.op("vector", "tensor_copy", xb[:, sl], xf[:, sl],
                              deps=(ld, st_tok[pp]))
            st_tok[pp] = p.dma("gpsimd", yr[i], xb[:, sl], deps=(cp_tok[pp],))
        p.op("gpsimd", "engine_nop", deps=(st_tok[(nt - 1) % 2], st_tok[nt % 2]))

    return build_program(fill)


def prep_weights(arrs):
    flat = np.concatenate([np.ascontiguousarray(a, np.float32).ravel() for a in arrs])
    unit = 128 * _PREP_COLS
    per = cdiv(len(flat), NCORES * unit) * unit
    flat = np.concatenate([flat, np.zeros(per * NCORES - len(flat), np.float32)])
    shards = flat.reshape(NCORES, per // _PREP_COLS, _PREP_COLS)
    nc = build_prep(per)
    res = _run(nc, [{"x": shards[i]} for i in range(NCORES)])
    out = np.concatenate([np.asarray(res[i]["y"]).ravel() for i in range(NCORES)])
    outs, off = [], 0
    for a in arrs:
        outs.append(np.ascontiguousarray(out[off:off + a.size].reshape(a.shape)))
        off += a.size
    return outs


def build_norm(kind):

    def fill(nc, p, stack):
        x = nc.dram_tensor("x", [TLOC, D], F32, kind="ExternalInput").ap()
        r = nc.dram_tensor("r", [TLOC, D], F32, kind="ExternalInput").ap() \
            if kind == "resid_ln" else None
        w = nc.dram_tensor("w", [1, D], F32, kind="ExternalInput").ap() \
            if kind == "rms" else None
        y = nc.dram_tensor("y", [TLOC, D], F32, kind="ExternalOutput").ap()
        nt = TLOC // 128
        xr = x.rearrange("(n p) c -> n p c", p=128)
        yr = y.rearrange("(n p) c -> n p c", p=128)
        rr = r.rearrange("(n p) c -> n p c", p=128) if r is not None else None
        xt = sb(nc, stack, "xt", [128, 2 * D], F32)
        rt = sb(nc, stack, "rt", [128, 2 * D], F32)
        sq = sb(nc, stack, "sq", [128, D], F32)
        stat = sb(nc, stack, "stat", [128, 8], F32)
        wt = sb(nc, stack, "wt", [128, D], F32)
        yt = sb(nc, stack, "yt", [128, 2 * D], F32)
        wtok = None
        if kind == "rms":
            wtok = p.dma("sync", wt[:], w.partition_broadcast(128).squeeze(1))
        st_tok = [None, None]
        use_tok = [None, None]
        for i in range(nt):
            pp = i % 2
            sl = slice(pp * D, (pp + 1) * D)
            xv, yv = xt[:, sl], yt[:, sl]
            ldx = p.dma("sync", xv, xr[i], deps=(use_tok[pp],))
            ldr = p.dma("sync", rt[:, sl], rr[i], deps=(use_tok[pp],)) \
                if r is not None else None
            if kind == "rms":
                t_sq = p.op("scalar", "activation", sq[:], xv, AF.Square, deps=(ldx,))
                t_v0 = p.op("vector", "tensor_reduce", stat[:, 0:1], sq[:],
                            AX.X, ALU.add, deps=(t_sq,))
                t_v1 = p.op("vector", "tensor_scalar", stat[:, 2:3],
                            stat[:, 0:1], 1.0 / D, EPS, op0=ALU.mult, op1=ALU.add)
                t_ln = p.op("scalar", "activation", stat[:, 4:5], stat[:, 2:3],
                            AF.Ln, deps=(t_v1,))
                t_rs = p.op("scalar", "activation", stat[:, 5:6], stat[:, 4:5],
                            AF.Exp, scale=-0.5)
                t_y0 = p.op("vector", "tensor_scalar", yv, xv, stat[:, 5:6], None,
                            op0=ALU.mult, deps=(t_rs, st_tok[pp]))
                t_y = p.op("vector", "tensor_mul", yv, yv, wt[:], deps=(wtok,))
                use_tok[pp] = t_y
            else:
                t_s0 = p.op("vector", "tensor_reduce", stat[:, 0:1], xv,
                            AX.X, ALU.add, deps=(ldx,))
                t_m = p.op("vector", "tensor_scalar_mul", stat[:, 1:2],
                           stat[:, 0:1], 1.0 / D)
                t_xc = p.op("vector", "tensor_scalar", yv, xv, stat[:, 1:2], None,
                            op0=ALU.subtract, deps=(st_tok[pp],))
                t_sq = p.op("scalar", "activation", sq[:], yv, AF.Square, deps=(t_xc,))
                t_v = p.op("vector", "tensor_reduce", stat[:, 2:3], sq[:],
                           AX.X, ALU.add, deps=(t_sq,))
                t_v2 = p.op("vector", "tensor_scalar", stat[:, 2:3],
                            stat[:, 2:3], 1.0 / D, EPS, op0=ALU.mult, op1=ALU.add)
                t_ln = p.op("scalar", "activation", stat[:, 4:5], stat[:, 2:3],
                            AF.Ln, deps=(t_v2,))
                t_rs = p.op("scalar", "activation", stat[:, 5:6], stat[:, 4:5],
                            AF.Exp, scale=-0.5)
                t_y = p.op("vector", "tensor_scalar", yv, yv, stat[:, 5:6], None,
                           op0=ALU.mult, deps=(t_rs,))
                if kind == "resid_ln":
                    t_y = p.op("vector", "tensor_add", yv, yv, rt[:, sl],
                               deps=(ldr,))
                use_tok[pp] = t_y
            st_tok[pp] = p.dma("gpsimd", yr[i], yv, deps=(use_tok[pp],))
        p.op("gpsimd", "engine_nop",
             deps=(st_tok[0], st_tok[1]))

    return build_program(fill)


_norm_cache = {}


def run_norm(kind, x_pos_full, r_pos_full=None, w=None):
    if kind not in _norm_cache:
        _norm_cache[kind] = build_norm(kind)
    nc = _norm_cache[kind]
    maps = []
    for c in range(NCORES):
        m = {"x": x_pos_full[c * TLOC:(c + 1) * TLOC]}
        if kind == "resid_ln":
            m["r"] = r_pos_full[c * TLOC:(c + 1) * TLOC]
        if kind == "rms":
            m["w"] = w.reshape(1, D)
        maps.append(m)
    res = _run(nc, maps)
    return np.concatenate([np.asarray(res[c]["y"]) for c in range(NCORES)], axis=0)


THW = 2051
TT_TILES = [(0, 3), (3, 512), (515, 512), (1027, 512), (1539, 512)]


def build_mamba_tok(variant):
    has_pro = variant in ("mid", "last")
    has_body = variant in ("first", "mid")

    def fill(nc, p, stack):
        u_in = nc.dram_tensor("u", [D, THW], F32, kind="ExternalInput").ap()
        if has_pro:
            yg = nc.dram_tensor("yg", [DIN, THW], BF16, kind="ExternalInput").ap()
            woutT = nc.dram_tensor("woutT", [DIN, D], BF16, kind="ExternalInput").ap()
            uout = nc.dram_tensor("uout", [D, TLOC], F32, kind="ExternalOutput").ap()
        if has_body:
            winT = nc.dram_tensor("winT", [D, 2 * DIN], BF16, kind="ExternalInput").ap()
            wxT = nc.dram_tensor("wxT", [DIN, 96], BF16, kind="ExternalInput").ap()
            wdtT = nc.dram_tensor("wdtT", [DTR, DIN], BF16, kind="ExternalInput").ap()
            convw = nc.dram_tensor("convw", [DIN, 4], F32, kind="ExternalInput").ap()
            convb = nc.dram_tensor("convb", [DIN, 1], F32, kind="ExternalInput").ap()
            bdt = nc.dram_tensor("bdt", [DIN, 1], F32, kind="ExternalInput").ap()
            delta_o = nc.dram_tensor("delta", [DIN, TLOC], F32, kind="ExternalOutput").ap()
            du_o = nc.dram_tensor("du", [DIN, TLOC], BF16, kind="ExternalOutput").ap()
            xin_o = nc.dram_tensor("xin", [DIN, TLOC], BF16, kind="ExternalOutput").ap()
            g_o = nc.dram_tensor("g", [DIN, TLOC], BF16, kind="ExternalOutput").ap()
            bc_o = nc.dram_tensor("bc", [96, TLOC], BF16, kind="ExternalOutput").ap()

        u_f = sb(nc, stack, "u_f", [128, 8 * 512], F32)
        u_ld = sb(nc, stack, "u_ld", [128, 8 * 512], F32)
        u_bf = sb(nc, stack, "u_bf", [128, 8 * 512], BF16)
        if has_pro:
            yg_sb = sb(nc, stack, "yg_sb", [128, 16 * 512], BF16)
            wout_sb = sb(nc, stack, "wout_sb", [128, 2 * 2048], BF16)
        if has_body:
            xzpre = sb(nc, stack, "xzpre", [128, 16 * THW], BF16)
            xin_sb = sb(nc, stack, "xin_sb", [128, 16 * 512], BF16)
            win_sb = sb(nc, stack, "win_sb", [128, 2 * 1024], BF16)
            wx_sb = sb(nc, stack, "wx_sb", [128, 16 * 96], BF16)
            wdt_sb = sb(nc, stack, "wdt_sb", [DTR, DIN], BF16)
            cw_sb = sb(nc, stack, "cw_sb", [128, 16 * 4], F32)
            cb_sb = sb(nc, stack, "cb_sb", [128, 16], F32)
            bdt_sb = sb(nc, stack, "bdt_sb", [128, 16], F32)
            xdbl = sb(nc, stack, "xdbl", [128, 512], BF16)
            acc1 = sb(nc, stack, "acc1", [128, 512], F32)
            acc2 = sb(nc, stack, "acc2", [128, 512], F32)
            tmp1 = sb(nc, stack, "tmp1", [128, 512], F32)
            tmp2 = sb(nc, stack, "tmp2", [128, 512], F32)
            tmp3 = sb(nc, stack, "tmp3", [128, 512], F32)
            dl_sb = sb(nc, stack, "dl_sb", [128, 2 * 512], F32)
            du_sb = sb(nc, stack, "du_sb", [128, 2 * 512], BF16)
            gz_sb = sb(nc, stack, "gz_sb", [128, 2 * 512], BF16)
        pA = [ps(nc, stack, "pA0", [128, 512]), ps(nc, stack, "pA1", [128, 512])] \
            if has_pro else None
        if has_body:
            pB = [ps(nc, stack, "pB0", [128, 512]), ps(nc, stack, "pB1", [128, 512])]
            pD96 = ps(nc, stack, "pD96", [96, 512])
            pE = [ps(nc, stack, "pE0", [128, 512]), ps(nc, stack, "pE1", [128, 512])]

        stat_tok = None
        if has_body:
            for k in range(16):
                stat_tok = p.dma("sync", wx_sb[:, k * 96:(k + 1) * 96],
                                 wxT[k * 128:(k + 1) * 128, :])
            stat_tok = p.dma("sync", wdt_sb[:], wdtT)
            for c in range(16):
                p.dma("sync", cw_sb[:, 4 * c:4 * c + 4],
                      convw[c * 128:(c + 1) * 128, :])
                p.dma("sync", cb_sb[:, c:c + 1], convb[c * 128:(c + 1) * 128, :])
                stat_tok = p.dma("sync", bdt_sb[:, c:c + 1],
                                 bdt[c * 128:(c + 1) * 128, :])

        uld_use = None
        ubf_use = None
        yg_use = None
        ufo_dma = [None] * 8
        evacA = [None, None]
        evacB = [None, None]
        psB_read = [None, None]
        evD_prev = None
        psE_read = [None, None]
        gz_dma = [None, None]
        dl_dma = [None, None]
        du_dma = [None, None]
        xin_dma = [None] * 16
        xin_readv = [None] * 16
        mmD_prev = None
        bc_dma = None
        tmp1_read = None
        xz_done = {}
        prev_wout_ld = [None, None]
        prev_win_ld = [None, None]

        for ti, (c0, w) in enumerate(TT_TILES):
            gcols = slice(c0, c0 + w)
            ocols = slice(c0 - 3, c0 - 3 + w)
            uld = p.dma("sync", u_ld[:, 0:8 * w].rearrange("p (o c) -> p o c", o=8),
                        u_in.rearrange("(o p) t -> p o t", p=128)[:, :, gcols],
                        deps=(uld_use,))
            if has_pro:
                ygld = []
                for k in range(16):
                    ygld.append(p.dma("sync", yg_sb[:, k * 512:k * 512 + w],
                                      yg[k * 128:(k + 1) * 128, gcols],
                                      deps=(yg_use,)))
                last_cv = None
                for o in range(8):
                    pp = o % 2
                    psum = pA[pp]
                    wld = p.dma("sync", wout_sb[:, pp * 2048:(pp + 1) * 2048]
                                .rearrange("p (k c) -> p k c", k=16),
                                woutT.rearrange("(k p) d -> p k d", p=128)
                                [:, :, o * 128:(o + 1) * 128],
                                deps=(prev_wout_ld[pp],))
                    mm_tok = None
                    for k in range(16):
                        mm_tok = p.mm(psum[:, 0:w],
                                      wout_sb[:, pp * 2048 + k * 128:
                                              pp * 2048 + (k + 1) * 128],
                                      yg_sb[:, k * 512:k * 512 + w],
                                      start=(k == 0), stop=(k == 15),
                                      deps=(wld, ygld[k], evacA[pp]),
                                      inc=(k == 15))
                    prev_wout_ld[pp] = mm_tok
                    ev = p.op("vector", "tensor_add", u_f[:, o * w:(o + 1) * w],
                              u_ld[:, o * w:(o + 1) * w], psum[:, 0:w],
                              deps=(mm_tok, uld, ufo_dma[o]))
                    evacA[pp] = ev
                    uld_use = ev
                    if has_body:
                        last_cv = p.op("vector", "tensor_copy",
                                       u_bf[:, o * w:(o + 1) * w],
                                       u_f[:, o * w:(o + 1) * w],
                                       deps=(ubf_use,))
                    if ti > 0:
                        ufo_dma[o] = p.dma(
                            "gpsimd",
                            uout.rearrange("(o p) t -> p o t", p=128)[:, o, ocols],
                            u_f[:, o * w:(o + 1) * w],
                            deps=(last_cv if has_body else ev,))
                yg_use = mm_tok
                ubf_last_write = last_cv
            else:
                cv = None
                for o in range(8):
                    cv = p.op("vector", "tensor_copy", u_bf[:, o * w:(o + 1) * w],
                              u_ld[:, o * w:(o + 1) * w], deps=(uld, ubf_use))
                uld_use = cv
                ubf_last_write = cv

            if not has_body:
                continue
            mmB = None
            for e in range(32):
                pp = e % 2
                psum = pB[pp]
                wld = p.dma("sync", win_sb[:, pp * 1024:(pp + 1) * 1024]
                            .rearrange("p (k c) -> p k c", k=8),
                            winT.rearrange("(k p) e -> p k e", p=128)
                            [:, :, e * 128:(e + 1) * 128],
                            deps=(prev_win_ld[pp],))
                for k in range(8):
                    mmB = p.mm(psum[:, 0:w],
                               win_sb[:, pp * 1024 + k * 128:pp * 1024 + (k + 1) * 128],
                               u_bf[:, k * w:(k + 1) * w],
                               start=(k == 0), stop=(k == 7),
                               deps=(wld, ubf_last_write, evacB[pp]),
                               inc=(k == 7))
                prev_win_ld[pp] = mmB
                if e < 16:
                    ev = p.op("vector", "tensor_copy",
                              xzpre[:, e * THW + c0: e * THW + c0 + w],
                              psum[:, 0:w], deps=(mmB,))
                    xz_done[(e, ti)] = ev
                else:
                    zp = e % 2
                    ev = p.op("scalar", "activation",
                              gz_sb[:, zp * 512:zp * 512 + w], psum[:, 0:w],
                              AF.Silu, deps=(mmB, gz_dma[zp]))
                    if ti > 0:
                        gz_dma[zp] = p.dma(
                            "gpsimd", g_o[(e - 16) * 128:(e - 16) * 128 + 128, ocols],
                            gz_sb[:, zp * 512:zp * 512 + w], deps=(ev,))
                evacB[pp] = ev
            ubf_use = mmB

            if ti == 0:
                continue
            xin_ready = []
            xs = None
            for c in range(16):
                base = c * THW
                sls = [slice(base + c0 - 3 + j, base + c0 - 3 + j + w)
                       for j in range(4)]
                dd = tuple(xz_done.get((c, t)) for t in range(ti + 1))
                p.op("vector", "tensor_scalar", acc1[:, 0:w], xzpre[:, sls[0]],
                     cw_sb[:, 4 * c:4 * c + 1], None, op0=ALU.mult,
                     deps=dd + (stat_tok, xs))
                p.op("vector", "scalar_tensor_tensor", acc2[:, 0:w],
                     xzpre[:, sls[1]], cw_sb[:, 4 * c + 1:4 * c + 2], acc1[:, 0:w],
                     ALU.mult, ALU.add)
                p.op("vector", "scalar_tensor_tensor", acc1[:, 0:w],
                     xzpre[:, sls[2]], cw_sb[:, 4 * c + 2:4 * c + 3], acc2[:, 0:w],
                     ALU.mult, ALU.add)
                t3 = p.op("vector", "scalar_tensor_tensor", acc2[:, 0:w],
                          xzpre[:, sls[3]], cw_sb[:, 4 * c + 3:4 * c + 4],
                          acc1[:, 0:w], ALU.mult, ALU.add)
                xs = p.op("scalar", "activation", xin_sb[:, c * 512:c * 512 + w],
                          acc2[:, 0:w], AF.Silu, bias=cb_sb[:, c:c + 1],
                          deps=(t3, xin_dma[c], mmD_prev, xin_readv[c]))
                xin_ready.append(xs)
                xin_dma[c] = p.dma("scalar", xin_o[c * 128:(c + 1) * 128, ocols],
                                   xin_sb[:, c * 512:c * 512 + w], deps=(xs,))
            mmD = None
            for k in range(16):
                mmD = p.mm(pD96[:, 0:w], wx_sb[:, k * 96:(k + 1) * 96],
                           xin_sb[:, k * 512:k * 512 + w],
                           start=(k == 0), stop=(k == 15),
                           deps=(xin_ready[k], evD_prev, stat_tok),
                           inc=(k == 15))
            mmD_prev = mmD
            evD = p.op("scalar", "activation", xdbl[0:96, 0:w], pD96[:, 0:w],
                       AF.Copy, deps=(mmD, bc_dma))
            evD_prev = evD
            bc_dma = p.dma("scalar", bc_o[:, ocols], xdbl[0:96, 0:w], deps=(evD,))
            for c in range(16):
                pp = c % 2
                psum = pE[pp]
                mmE = p.mm(psum[:, 0:w], wdt_sb[:, c * 128:(c + 1) * 128],
                           xdbl[0:DTR, 0:w], start=True, stop=True,
                           deps=(evD, psE_read[pp]), inc=True)
                s0 = p.op("scalar", "activation", tmp1[:, 0:w], psum[:, 0:w],
                          AF.Abs, bias=bdt_sb[:, c:c + 1],
                          deps=(mmE, tmp1_read))
                p.op("scalar", "activation", tmp2[:, 0:w], tmp1[:, 0:w],
                     AF.Exp, scale=-1.0)
                p.op("scalar", "activation", tmp3[:, 0:w], tmp2[:, 0:w],
                     AF.Ln, bias=1.0)
                s3 = p.op("scalar", "activation", tmp1[:, 0:w], psum[:, 0:w],
                          AF.Relu, bias=bdt_sb[:, c:c + 1])
                psE_read[pp] = s3
                v0 = p.op("vector", "tensor_add", dl_sb[:, pp * 512:pp * 512 + w],
                          tmp1[:, 0:w], tmp3[:, 0:w], deps=(s3, dl_dma[pp]))
                tmp1_read = v0
                dl_dma[pp] = p.dma("gpsimd", delta_o[c * 128:(c + 1) * 128, ocols],
                                   dl_sb[:, pp * 512:pp * 512 + w], deps=(v0,))
                v1 = p.op("vector", "tensor_mul", du_sb[:, pp * 512:pp * 512 + w],
                          dl_sb[:, pp * 512:pp * 512 + w],
                          xin_sb[:, c * 512:c * 512 + w], deps=(du_dma[pp],))
                xin_readv[c] = v1
                du_dma[pp] = p.dma("gpsimd", du_o[c * 128:(c + 1) * 128, ocols],
                                   du_sb[:, pp * 512:pp * 512 + w], deps=(v1,))

        p.op("gpsimd", "engine_nop",
             deps=(p.last_tok["gpsimd"], p.last_tok["scalar"]))

    return build_program(fill)


SCH = 1024
NCH = S // SCH


def build_mamba_scan():
    def fill(nc, p, stack):
        TS = B * S
        dsl = nc.dram_tensor("dsl", [CSL, TS], F32, kind="ExternalInput").ap()
        dusl = nc.dram_tensor("dusl", [CSL, TS], BF16, kind="ExternalInput").ap()
        xinsl = nc.dram_tensor("xinsl", [CSL, TS], BF16, kind="ExternalInput").ap()
        gsl = nc.dram_tensor("gsl", [CSL, TS], BF16, kind="ExternalInput").ap()
        alog = nc.dram_tensor("alog", [CSL, NST], F32, kind="ExternalInput").ap()
        dpar = nc.dram_tensor("dpar", [CSL, 1], F32, kind="ExternalInput").ap()
        bfm = nc.dram_tensor("bfm", [NST, TS], BF16, kind="ExternalInput").ap()
        cpos = nc.dram_tensor("cpos", [1, TS * NST], BF16, kind="ExternalInput").ap()
        ygo = nc.dram_tensor("ygsl", [CSL, TS], BF16, kind="ExternalOutput").ap()

        d_sb = sb(nc, stack, "d_sb", [128, 2 * SCH], F32)
        du_sb = sb(nc, stack, "du_sb", [128, 2 * SCH], BF16)
        xin_sb = sb(nc, stack, "xin_sb", [128, 2 * SCH], BF16)
        g_sb = sb(nc, stack, "g_sb", [128, 2 * SCH], BF16)
        a_sb = sb(nc, stack, "a_sb", [128, 2 * SCH], F32)
        b_sb = sb(nc, stack, "b_sb", [128, 2 * SCH], BF16)
        H = sb(nc, stack, "H", [128, NST * SCH], BF16)
        brep = sb(nc, stack, "brep", [128, NST * SCH], BF16)
        crep = sb(nc, stack, "crep", [128, NST * SCH], BF16)
        y1 = sb(nc, stack, "y1", [128, 8 * SCH], BF16)
        y1b = sb(nc, stack, "y1b", [128, 8 * SCH], BF16)
        y2 = sb(nc, stack, "y2", [128, 4 * SCH], BF16)
        y2b = sb(nc, stack, "y2b", [128, 4 * SCH], BF16)
        ygout = sb(nc, stack, "ygout", [128, 2 * SCH], BF16)
        al_sb = sb(nc, stack, "al_sb", [128, 2 * NST], F32)
        A_sb = sb(nc, stack, "A_sb", [128, 2 * NST], F32)
        dp_sb = sb(nc, stack, "dp_sb", [128, 2], F32)
        state = sb(nc, stack, "state", [128, 2 * NST], F32)
        y3 = y1[:].bitcast(F32)
        y5 = y1b[:].bitcast(F32)
        y4 = y2[:].bitcast(F32)
        yy = y2b[:].bitcast(F32)

        t_al = p.dma("sync", al_sb[:, 0:NST], alog[0:128, :])
        t_al = p.dma("sync", al_sb[:, NST:2 * NST], alog[128:256, :])
        t_dp = p.dma("sync", dp_sb[:, 0:1], dpar[0:128, :])
        t_dp = p.dma("sync", dp_sb[:, 1:2], dpar[128:256, :])
        t_A0 = p.op("scalar", "activation", A_sb[:], al_sb[:], AF.Exp, deps=(t_al,))
        t_A = p.op("vector", "tensor_scalar_mul", A_sb[:], A_sb[:], -1.0,
                   deps=(t_A0,))

        def v3(t, n):
            return t.rearrange("p (a b) -> p a b", b=n)

        ld_use = [None, None]
        brep_use = None
        crep_use = None
        yg_dma = [None, None]
        st_tok = [None, None]
        last_tail = None

        for bi in range(B):
            st_tok[0] = p.op("gpsimd", "memset", state[:, 0:NST], 0.0,
                             deps=(st_tok[0],))
            st_tok[1] = p.op("gpsimd", "memset", state[:, NST:2 * NST], 0.0,
                             deps=(st_tok[1],))
            for tch in range(NCH):
                t0 = bi * S + tch * SCH
                tsl = slice(t0, t0 + SCH)
                btoks = []
                for n in range(NST):
                    btoks.append(p.dma(
                        "sync", brep[:, n * SCH:(n + 1) * SCH],
                        bfm[n:n + 1, tsl].partition_broadcast(128).squeeze(1),
                        deps=(brep_use,)))
                ctok = p.dma(
                    "sync", crep[:],
                    cpos[0:1, t0 * NST:(t0 + SCH) * NST]
                    .partition_broadcast(128).squeeze(1),
                    deps=(crep_use,))
                for ct in range(2):
                    pp = ct
                    cs = slice(ct * 128, (ct + 1) * 128)
                    csl = slice(pp * SCH, (pp + 1) * SCH)
                    ld_d = p.dma("sync", d_sb[:, csl], dsl[cs, tsl],
                                 deps=(ld_use[pp],))
                    ld_du = p.dma("sync", du_sb[:, csl], dusl[cs, tsl],
                                  deps=(ld_use[pp],))
                    ld_xin = p.dma("sync", xin_sb[:, csl], xinsl[cs, tsl],
                                   deps=(ld_use[pp],))
                    ld_g = p.dma("sync", g_sb[:, csl], gsl[cs, tsl],
                                 deps=(ld_use[pp],))
                    tsts = []
                    for n in range(NST):
                        np_ = n % 2
                        asl_ = slice(np_ * SCH, (np_ + 1) * SCH)
                        awar = tsts[n - 2] if n >= 2 else last_tail
                        ta = p.op("scalar", "activation", a_sb[:, asl_],
                                  d_sb[:, csl], AF.Exp,
                                  scale=A_sb[:, ct * NST + n:ct * NST + n + 1],
                                  deps=(ld_d, t_A, awar))
                        tb = p.op("vector", "tensor_mul", b_sb[:, asl_],
                                  du_sb[:, csl], brep[:, n * SCH:(n + 1) * SCH],
                                  deps=(ld_du, btoks[n]))
                        tsc = p.op("vector", "tensor_tensor_scan",
                                   v3(H[:], NST)[:, :, n],
                                   a_sb[:, asl_], b_sb[:, asl_],
                                   state[:, ct * NST + n:ct * NST + n + 1],
                                   ALU.mult, ALU.add,
                                   deps=(ta, st_tok[ct]))
                        tst = p.op("vector", "tensor_copy",
                                   state[:, ct * NST + n:ct * NST + n + 1],
                                   H[:, (SCH - 1) * NST + n:(SCH - 1) * NST + n + 1],
                                   deps=(tsc,))
                        tsts.append(tst)
                    st_tok[ct] = tsts[-1]
                    last_tail = tsts[-1]
                    m1 = p.op("vector", "tensor_mul", v3(y1[:], 8),
                              v3(H[:], NST)[:, :, 0:8],
                              v3(crep[:], NST)[:, :, 0:8],
                              deps=(ctok,))
                    m2 = p.op("vector", "tensor_mul", v3(y1b[:], 8),
                              v3(H[:], NST)[:, :, 8:16],
                              v3(crep[:], NST)[:, :, 8:16])
                    p.op("vector", "tensor_add", v3(y2[:], 4),
                         v3(y1[:], 8)[:, :, 0:4], v3(y1[:], 8)[:, :, 4:8])
                    p.op("vector", "tensor_add", v3(y2b[:], 4),
                         v3(y1b[:], 8)[:, :, 0:4], v3(y1b[:], 8)[:, :, 4:8])
                    p.op("vector", "tensor_add", v3(y3, 4),
                         v3(y2[:], 4), v3(y2b[:], 4))
                    p.op("vector", "tensor_add", v3(y5, 4)[:, 0:SCH, 0:2],
                         v3(y3, 4)[:, :, 0:2], v3(y3, 4)[:, :, 2:4])
                    a4 = p.op("vector", "tensor_add", v3(y4, 2)[:, 0:SCH, 0],
                              v3(y5, 4)[:, 0:SCH, 0], v3(y5, 4)[:, 0:SCH, 1])
                    a5 = p.op("vector", "scalar_tensor_tensor",
                              v3(yy, 2)[:, 0:SCH, 0],
                              xin_sb[:, csl], dp_sb[:, ct:ct + 1],
                              v3(y4, 2)[:, 0:SCH, 0],
                              ALU.mult, ALU.add, deps=(ld_xin, t_dp))
                    a6 = p.op("vector", "tensor_mul", ygout[:, csl],
                              v3(yy, 2)[:, 0:SCH, 0],
                              g_sb[:, csl], deps=(ld_g, yg_dma[pp]))
                    ld_use[pp] = a6
                    yg_dma[pp] = p.dma("gpsimd", ygo[cs, tsl], ygout[:, csl],
                                       deps=(a6,))
                brep_use = last_tail
                crep_use = m2
        p.op("gpsimd", "engine_nop", deps=(p.last_tok["gpsimd"],))

    return build_program(fill)


_prog_cache = {}


def get_prog(name, builder, *args):
    if name not in _prog_cache:
        _prog_cache[name] = builder(*args)
    return _prog_cache[name]


def halo(full, c):
    q, b = c % 4, c // 4
    s0 = b * S + q * TLOC
    if q == 0:
        z = np.zeros((full.shape[0], 3), full.dtype)
        return np.ascontiguousarray(
            np.concatenate([z, full[:, s0:s0 + TLOC]], axis=1))
    return np.ascontiguousarray(full[:, s0 - 3:s0 + TLOC])


def run_mamba_stack(u_fm, wb, raw):
    yg = None
    for i in range(NLAYER):
        maps = []
        for c in range(NCORES):
            m = {"u": halo(u_fm, c),
                 "winT": wb["winT"][i], "wxT": wb["wxT"][i],
                 "wdtT": wb["wdtT"][i],
                 "convw": np.ascontiguousarray(raw["conv_w"][i]),
                 "convb": np.ascontiguousarray(raw["conv_b"][i].reshape(DIN, 1)),
                 "bdt": np.ascontiguousarray(raw["b_dt"][i].reshape(DIN, 1))}
            if i > 0:
                m["yg"] = halo(yg, c)
                m["woutT"] = wb["woutT"][i - 1]
            maps.append(m)
        prog = get_prog("tok_first" if i == 0 else "tok_mid",
                        build_mamba_tok, "first" if i == 0 else "mid")
        res = _run(prog, maps)
        delta = np.concatenate([np.asarray(res[c]["delta"]) for c in range(NCORES)], 1)
        du = np.concatenate([np.asarray(res[c]["du"]) for c in range(NCORES)], 1)
        xin = np.concatenate([np.asarray(res[c]["xin"]) for c in range(NCORES)], 1)
        g = np.concatenate([np.asarray(res[c]["g"]) for c in range(NCORES)], 1)
        bc = np.concatenate([np.asarray(res[c]["bc"]) for c in range(NCORES)], 1)
        if i > 0:
            u_fm = np.concatenate([np.asarray(res[c]["uout"])
                                   for c in range(NCORES)], 1)
        bfm = np.ascontiguousarray(bc[64:80])
        cpos = np.ascontiguousarray(bc[80:96].T).reshape(1, B * S * NST)
        smaps = []
        for c in range(NCORES):
            rs = slice(c * CSL, (c + 1) * CSL)
            smaps.append(dict(
                dsl=np.ascontiguousarray(delta[rs]),
                dusl=np.ascontiguousarray(du[rs]),
                xinsl=np.ascontiguousarray(xin[rs]),
                gsl=np.ascontiguousarray(g[rs]),
                alog=np.ascontiguousarray(raw["A_log"][i][rs]),
                dpar=np.ascontiguousarray(raw["D_par"][i][rs].reshape(CSL, 1)),
                bfm=bfm, cpos=cpos))
        sres = _run(get_prog("scan", build_mamba_scan), smaps)
        yg = np.concatenate([np.asarray(sres[c]["ygsl"]) for c in range(NCORES)], 0)
    maps = [{"u": halo(u_fm, c), "yg": halo(yg, c), "woutT": wb["woutT"][NLAYER - 1]}
            for c in range(NCORES)]
    res = _run(get_prog("tok_last", build_mamba_tok, "last"), maps)
    return np.concatenate([np.asarray(res[c]["uout"]) for c in range(NCORES)], 1)


def build_l0ffn():
    def fill(nc, p, stack):
        v_in = nc.dram_tensor("v", [D, TLOC], F32, kind="ExternalInput").ap()
        w1T = nc.dram_tensor("w1T", [D, 4096], BF16, kind="ExternalInput").ap()
        w2T = nc.dram_tensor("w2T", [D, 4096], BF16, kind="ExternalInput").ap()
        h1T = nc.dram_tensor("h1T", [4096, 4096], BF16, kind="ExternalInput").ap()
        h2T = nc.dram_tensor("h2T", [4096, 4096], BF16, kind="ExternalInput").ap()
        w3T = nc.dram_tensor("w3T", [4096, D], BF16, kind="ExternalInput").ap()
        s1 = nc.dram_tensor("s1", [4096, 1], F32, kind="ExternalInput").ap()
        s2 = nc.dram_tensor("s2", [4096, 1], F32, kind="ExternalInput").ap()
        sh1 = nc.dram_tensor("sh1", [4096, 1], F32, kind="ExternalInput").ap()
        sh2 = nc.dram_tensor("sh2", [4096, 1], F32, kind="ExternalInput").ap()
        s3 = nc.dram_tensor("s3", [D, 1], F32, kind="ExternalInput").ap()
        f_o = nc.dram_tensor("f", [D, TLOC], F32, kind="ExternalOutput").ap()

        v_ld = sb(nc, stack, "v_ld", [128, 8 * 512], F32)
        v_bf = sb(nc, stack, "v_bf", [128, 8 * 512], BF16)
        hA = sb(nc, stack, "hA", [128, 32 * 512], BF16)
        hB = sb(nc, stack, "hB", [128, 32 * 512], BF16)
        hC = sb(nc, stack, "hC", [128, 32 * 512], BF16)
        fo_sb = sb(nc, stack, "fo_sb", [128, 2 * 512], F32)
        wst = sb(nc, stack, "wst", [128, 2 * 4096], BF16)
        sc_sb = sb(nc, stack, "sc_sb", [128, 4 * 32 + 8], F32)
        pm = [ps(nc, stack, "pm0", [128, 512]), ps(nc, stack, "pm1", [128, 512])]

        stok = None
        for o in range(32):
            p.dma("sync", sc_sb[:, o:o + 1], s1[o * 128:(o + 1) * 128, :])
            p.dma("sync", sc_sb[:, 32 + o:33 + o], s2[o * 128:(o + 1) * 128, :])
            p.dma("sync", sc_sb[:, 64 + o:65 + o], sh1[o * 128:(o + 1) * 128, :])
            stok = p.dma("sync", sc_sb[:, 96 + o:97 + o], sh2[o * 128:(o + 1) * 128, :])
        for o in range(8):
            stok = p.dma("sync", sc_sb[:, 128 + o:129 + o], s3[o * 128:(o + 1) * 128, :])

        vuse = None
        wld_prev = [None, None]
        evac = [None, None]
        fo_dma = [None, None]

        def dense(out_sb, out_blocks, in_sb, in_blocks, wT, scale_col,
                  func, w, in_tok, out_war=None, ocols=None):
            toks = []
            mm = None
            for o in range(out_blocks):
                pp = o % 2
                wld = p.dma("sync", wst[:, pp * 4096:pp * 4096 + in_blocks * 128]
                            .rearrange("p (k c) -> p k c", k=in_blocks),
                            wT.rearrange("(k p) e -> p k e", p=128)
                            [:, :, o * 128:(o + 1) * 128],
                            deps=(wld_prev[pp],))
                for k in range(in_blocks):
                    mm = p.mm(pm[pp][:, 0:w],
                              wst[:, pp * 4096 + k * 128:pp * 4096 + (k + 1) * 128],
                              in_sb[:, k * 512:k * 512 + w],
                              start=(k == 0), stop=(k == in_blocks - 1),
                              deps=(wld, evac[pp], stok, in_tok),
                              inc=(k == in_blocks - 1))
                wld_prev[pp] = mm
                if out_sb is None:
                    ev = p.op("vector", "tensor_scalar",
                              fo_sb[:, pp * 512:pp * 512 + w], pm[pp][:, 0:w],
                              sc_sb[:, scale_col + o:scale_col + o + 1], None,
                              op0=ALU.mult, deps=(mm, fo_dma[pp]))
                    fo_dma[pp] = p.dma("gpsimd", f_o.rearrange(
                        "(o p) t -> p o t", p=128)[:, o, ocols],
                        fo_sb[:, pp * 512:pp * 512 + w], deps=(ev,))
                elif func == AF.Copy:
                    ev = p.op("vector", "tensor_scalar",
                              out_sb[:, o * 512:o * 512 + w], pm[pp][:, 0:w],
                              sc_sb[:, scale_col + o:scale_col + o + 1], None,
                              op0=ALU.mult, deps=(mm, out_war))
                else:
                    ev = p.op("scalar", "activation",
                              out_sb[:, o * 512:o * 512 + w], pm[pp][:, 0:w], func,
                              scale=sc_sb[:, scale_col + o:scale_col + o + 1],
                              deps=(mm, out_war))
                toks.append(ev)
                evac[pp] = ev
            return toks, mm

        hC_read = None
        for ti in range(4):
            w = 512
            tcols = slice(ti * 512, (ti + 1) * 512)
            ld = p.dma("sync", v_ld[:, 0:8 * 512].rearrange("p (o c) -> p o c", o=8),
                       v_in.rearrange("(o p) t -> p o t", p=128)[:, :, tcols],
                       deps=(vuse,))
            cv = None
            for o in range(8):
                cv = p.op("vector", "tensor_copy", v_bf[:, o * 512:(o + 1) * 512],
                          v_ld[:, o * 512:(o + 1) * 512], deps=(ld, vuse))
            t_h1, _ = dense(hA, 32, v_bf, 8, w1T, 0, AF.Silu, w, cv)
            t_g2, mm_g2 = dense(hB, 32, v_bf, 8, w2T, 32, AF.Copy, w, cv)
            vuse = mm_g2
            t_h = None
            for o in range(32):
                t_h = p.op("vector", "tensor_mul", hC[:, o * 512:o * 512 + w],
                           hA[:, o * 512:o * 512 + w], hB[:, o * 512:o * 512 + w],
                           deps=(t_h1[o], t_g2[o], hC_read))
            t_h2, mm_h2 = dense(hA, 32, hC, 32, h1T, 64, AF.Gelu, w, t_h,
                                out_war=t_h)
            hC_read = mm_h2
            t_h3, _ = dense(hB, 32, hA, 32, h2T, 96, AF.Gelu, w, t_h2[-1],
                            out_war=t_h)
            dense(None, 8, hB, 32, w3T, 128, AF.Copy, w, t_h3[-1], ocols=tcols)
        p.op("gpsimd", "engine_nop",
             deps=(p.last_tok["scalar"], p.last_tok["gpsimd"]))

    return build_program(fill)


def build_l1attn():
    def fill(nc, p, stack):
        v_in = nc.dram_tensor("v", [D, TLOC], F32, kind="ExternalInput").ap()
        wT = nc.dram_tensor("wT", [D, D], BF16, kind="ExternalInput").ap()
        bias = nc.dram_tensor("bias", [D, 1], F32, kind="ExternalInput").ap()
        g_o = nc.dram_tensor("g", [D, TLOC], F32, kind="ExternalOutput").ap()

        v_ld = sb(nc, stack, "v_ld", [128, 8 * 512], F32)
        v_bf = sb(nc, stack, "v_bf", [128, 8 * 512], BF16)
        h_sb = sb(nc, stack, "h_sb", [128, 8 * TLOC], BF16)
        ones = sb(nc, stack, "ones", [128, 128], BF16)
        triu = sb(nc, stack, "triu", [128, 128], BF16)
        b_sb = sb(nc, stack, "b_sb", [128, 8], F32)
        wst = sb(nc, stack, "wst", [128, 2 * 1024], BF16)
        go_sb = sb(nc, stack, "go_sb", [128, 2 * 512], F32)
        pm = [ps(nc, stack, "pm0", [128, 512]), ps(nc, stack, "pm1", [128, 512])]

        t_on = p.op("gpsimd", "memset", ones[:], 1.0)
        t_tr = p.op("gpsimd", "memset", triu[:], 1.0)
        t_tr = p.op("gpsimd", "affine_select", triu[:], triu[:], [[1, 128]],
                    ALU.is_ge, 0.0, base=0, channel_multiplier=-1, deps=(t_tr,))
        btok = None
        for o in range(8):
            btok = p.dma("sync", b_sb[:, o:o + 1], bias[o * 128:(o + 1) * 128, :])

        vuse = None
        wld_prev = [None, None]
        evac = [None, None]
        go_dma = [None, None]
        h_done = []
        for ti in range(4):
            w = 512
            tcols = slice(ti * 512, (ti + 1) * 512)
            ld = p.dma("sync", v_ld[:].rearrange("p (o c) -> p o c", o=8),
                       v_in.rearrange("(o p) t -> p o t", p=128)[:, :, tcols],
                       deps=(vuse,))
            cv = None
            for o in range(8):
                cv = p.op("vector", "tensor_copy", v_bf[:, o * 512:(o + 1) * 512],
                          v_ld[:, o * 512:(o + 1) * 512], deps=(ld, vuse))
            for o in range(8):
                pp = o % 2
                wld = p.dma("sync", wst[:, pp * 1024:(pp + 1) * 1024]
                            .rearrange("p (k c) -> p k c", k=8),
                            wT.rearrange("(k p) e -> p k e", p=128)
                            [:, :, o * 128:(o + 1) * 128],
                            deps=(wld_prev[pp],))
                mm = None
                for k in range(8):
                    mm = p.mm(pm[pp][:, 0:w],
                              wst[:, pp * 1024 + k * 128:pp * 1024 + (k + 1) * 128],
                              v_bf[:, k * 512:k * 512 + w],
                              start=(k == 0), stop=(k == 7),
                              deps=(wld, cv, evac[pp]), inc=(k == 7))
                wld_prev[pp] = mm
                ev = p.op("vector", "tensor_scalar",
                          h_sb[:, o * TLOC + ti * 512:o * TLOC + ti * 512 + w],
                          pm[pp][:, 0:w], b_sb[:, o:o + 1], None, op0=ALU.add,
                          deps=(mm, btok))
                evac[pp] = ev
                h_done.append(ev)
            vuse = mm
        hd = tuple(h_done)
        for mo in range(8):
            for ti in range(4):
                w = 512
                pp = (mo * 4 + ti) % 2
                mm = None
                for ko in range(mo + 1):
                    lhs = triu[:] if ko == mo else ones[:]
                    mm = p.mm(pm[pp][:, 0:w], lhs,
                              h_sb[:, ko * TLOC + ti * 512:ko * TLOC + ti * 512 + w],
                              start=(ko == 0), stop=(ko == mo),
                              deps=hd + (t_on, t_tr, evac[pp]), inc=(ko == mo))
                ev = p.op("vector", "tensor_copy", go_sb[:, pp * 512:pp * 512 + w],
                          pm[pp][:, 0:w], deps=(mm, go_dma[pp]))
                evac[pp] = ev
                go_dma[pp] = p.dma("gpsimd",
                                   g_o.rearrange("(o p) t -> p o t", p=128)
                                   [:, mo, ti * 512:ti * 512 + w],
                                   go_sb[:, pp * 512:pp * 512 + w], deps=(ev,))
        p.op("gpsimd", "engine_nop", deps=(p.last_tok["gpsimd"],))

    return build_program(fill)


def build_qkv():
    def fill(nc, p, stack):
        x_in = nc.dram_tensor("x", [D, TLOC], F32, kind="ExternalInput").ap()
        wqT = nc.dram_tensor("wqT", [D, D], BF16, kind="ExternalInput").ap()
        wkT = nc.dram_tensor("wkT", [D, D], BF16, kind="ExternalInput").ap()
        wvT = nc.dram_tensor("wvT", [D, D], BF16, kind="ExternalInput").ap()
        q_o = nc.dram_tensor("q", [D, TLOC], BF16, kind="ExternalOutput").ap()
        k_o = nc.dram_tensor("k", [D, TLOC], BF16, kind="ExternalOutput").ap()
        v_o = nc.dram_tensor("vpos", [TLOC, D], BF16, kind="ExternalOutput").ap()

        x_ld = sb(nc, stack, "x_ld", [128, 8 * 256], F32)
        x_bf = sb(nc, stack, "x_bf", [128, 8 * TLOC], BF16)
        wst = sb(nc, stack, "wst", [128, 2 * 1024], BF16)
        wv_sb = sb(nc, stack, "wv_sb", [128, 8 * 1024], BF16)
        o_sb = sb(nc, stack, "o_sb", [128, 4 * 512], BF16)
        vo_sb = sb(nc, stack, "vo_sb", [128, 2 * 1024], BF16)
        pm = [ps(nc, stack, "pm0", [128, 512]), ps(nc, stack, "pm1", [128, 512])]
        pv = [ps(nc, stack, "pv0", [128, 512]), ps(nc, stack, "pv1", [128, 512])]

        cv = None
        xuse = None
        for i in range(64):
            pp = i % 2
            sl = slice(pp * 8 * 128, pp * 8 * 128 + 256) if False else None
            ld = p.dma("sync", x_ld[:, (i % 8) * 256:(i % 8) * 256 + 256],
                       x_in.rearrange("(o p) t -> p o t", p=128)
                       [:, i // 8, (i % 8) * 256:(i % 8) * 256 + 256],
                       deps=(cv,))
            cv = p.op("vector", "tensor_copy",
                      x_bf[:, (i // 8) * TLOC + (i % 8) * 256:
                           (i // 8) * TLOC + (i % 8) * 256 + 256],
                      x_ld[:, (i % 8) * 256:(i % 8) * 256 + 256], deps=(ld,))
        xtok = cv
        wvtok = None
        for k in range(8):
            wvtok = p.dma("sync", wv_sb[:, k * 1024:(k + 1) * 1024],
                          wvT[k * 128:(k + 1) * 128, :])

        wld_prev = [None, None]
        evac = [None, None]
        out_dma = [None, None, None, None]
        for wi, (wT, dst) in enumerate(((wqT, q_o), (wkT, k_o))):
            for o in range(8):
                wp = o % 2
                wld = p.dma("sync", wst[:, wp * 1024:(wp + 1) * 1024]
                            .rearrange("p (k c) -> p k c", k=8),
                            wT.rearrange("(k p) e -> p k e", p=128)
                            [:, :, o * 128:(o + 1) * 128],
                            deps=(wld_prev[wp],))
                for ti in range(4):
                    w = 512
                    pp = ti % 2
                    sp = wi * 2 + pp
                    mm = None
                    for k in range(8):
                        mm = p.mm(pm[pp][:, 0:w],
                                  wst[:, wp * 1024 + k * 128:wp * 1024 + (k + 1) * 128],
                                  x_bf[:, k * TLOC + ti * 512:k * TLOC + ti * 512 + w],
                                  start=(k == 0), stop=(k == 7),
                                  deps=(wld, xtok, evac[pp]), inc=(k == 7))
                    if ti == 3:
                        wld_prev[wp] = mm
                    ev = p.op("vector", "tensor_copy", o_sb[:, sp * 512:sp * 512 + w],
                              pm[pp][:, 0:w], deps=(mm, out_dma[sp]))
                    evac[pp] = ev
                    out_dma[sp] = p.dma("gpsimd",
                                        dst[o * 128:(o + 1) * 128,
                                            ti * 512:ti * 512 + w],
                                        o_sb[:, sp * 512:sp * 512 + w], deps=(ev,))
        vevac = [None, None]
        vdma = [None, None]
        for tc in range(16):
            pp = tc % 2
            mm = None
            for half in range(2):
                for k in range(8):
                    mm = p.mm(pv[pp][:, 0:512],
                              x_bf[:, k * TLOC + tc * 128:k * TLOC + (tc + 1) * 128],
                              wv_sb[:, k * 1024 + half * 512:k * 1024 + half * 512 + 512],
                              start=(k == 0), stop=(k == 7),
                              deps=(wvtok, xtok, vevac[pp]), inc=(k == 7))
                ev = p.op("vector", "tensor_copy",
                          vo_sb[:, pp * 1024 + half * 512:pp * 1024 + half * 512 + 512],
                          pv[pp][:, 0:512], deps=(mm, vdma[pp]))
                vevac[pp] = ev
            vdma[pp] = p.dma("gpsimd", v_o[tc * 128:(tc + 1) * 128, :],
                             vo_sb[:, pp * 1024:pp * 1024 + 1024], deps=(vevac[pp],))
        p.op("gpsimd", "engine_nop", deps=(p.last_tok["gpsimd"],))

    return build_program(fill)


NKC = S // 128


def build_attn():
    def fill(nc, p, stack):
        q_in = nc.dram_tensor("q", [D, TLOC], BF16, kind="ExternalInput").ap()
        k_in = nc.dram_tensor("kfull", [D, S], BF16, kind="ExternalInput").ap()
        v_in = nc.dram_tensor("vfull", [S, D], BF16, kind="ExternalInput").ap()
        woT = nc.dram_tensor("woT", [D, D], BF16, kind="ExternalInput").ap()
        y1 = nc.dram_tensor("y1", [D, TLOC], F32, kind="ExternalInput").ap()
        h_o = nc.dram_tensor("h", [D, TLOC], F32, kind="ExternalOutput").ap()

        q_sb = sb(nc, stack, "q_sb", [64, 16 * TLOC], BF16)
        kh_sb = sb(nc, stack, "kh_sb", [64, 2 * S], BF16)
        vh_sb = sb(nc, stack, "vh_sb", [128, 2 * 65 * NKC], BF16)
        pt_sb = sb(nc, stack, "pt_sb", [128, 2 * 512], BF16)
        ao_sb = sb(nc, stack, "ao_sb", [128, 8 * TLOC], BF16)
        den_st = sb(nc, stack, "den_st", [1, 2 * TLOC], F32)
        den_rc = sb(nc, stack, "den_rc", [1, 2 * TLOC], F32)
        rdrep = sb(nc, stack, "rdrep", [128, TLOC], F32)
        mb_sb = sb(nc, stack, "mb_sb", [128, 1], F32)
        y1_ld = sb(nc, stack, "y1_ld", [128, 2 * 512], F32)
        ho_sb = sb(nc, stack, "ho_sb", [128, 2 * 512], F32)
        wst = sb(nc, stack, "wst", [128, 2 * 1024], BF16)
        rd_dram = nc.dram_tensor("rd_scratch", [16, TLOC], F32, kind="Internal").ap()

        pqk = [ps(nc, stack, "pqk0", [128, 512]), ps(nc, stack, "pqk1", [128, 512])]
        ppv = [ps(nc, stack, "ppv0", [65, 512]), ps(nc, stack, "ppv1", [65, 512])]
        pwo = [ps(nc, stack, "pwo0", [128, 512]), ps(nc, stack, "pwo1", [128, 512])]

        t_mb = p.op("gpsimd", "memset", mb_sb[:], -40.0)
        qtok = None
        for h in range(16):
            qtok = p.dma("sync", q_sb[:, h * TLOC:(h + 1) * TLOC],
                         q_in[h * 64:(h + 1) * 64, :])

        kh_use = [None, None]
        vh_use = [None, None]
        rd_dma = [None, None]
        pt_use = [None, None]
        qk_evac = [None, None]
        pv_evac = [None, None]
        den_cp = None
        ao_war = None
        for h in range(16):
            hp = h % 2
            kld = p.dma("sync", kh_sb[:, hp * S:(hp + 1) * S],
                        k_in[h * 64:(h + 1) * 64, :], deps=(kh_use[hp],))
            vms = p.op("gpsimd", "memset",
                       vh_sb[:, hp * 65 * NKC:(hp + 1) * 65 * NKC], 1.0,
                       deps=(vh_use[hp],))
            vld = None
            for ch in range(NKC):
                vld = p.dma("sync",
                            vh_sb[:, hp * 65 * NKC + ch * 65:
                                  hp * 65 * NKC + ch * 65 + 64],
                            v_in[ch * 128:(ch + 1) * 128, h * 64:(h + 1) * 64],
                            deps=(vms,))
            for tqi in range(4):
                w = 512
                pp = tqi % 2
                mmpv = None
                for ch in range(NKC):
                    cp = ch % 2
                    mmqk = p.mm(pqk[cp][:, 0:w],
                                kh_sb[:, hp * S + ch * 128:hp * S + (ch + 1) * 128],
                                q_sb[:, h * TLOC + tqi * 512:
                                     h * TLOC + tqi * 512 + w],
                                start=True, stop=True,
                                deps=(kld, qtok, qk_evac[cp]), inc=True)
                    ex = p.op("scalar", "activation", pt_sb[:, cp * 512:cp * 512 + w],
                              pqk[cp][:, 0:w], AF.Exp, scale=0.125,
                              bias=mb_sb[:, 0:1],
                              deps=(mmqk, t_mb, pt_use[cp]))
                    qk_evac[cp] = ex
                    mmpv = p.mm(ppv[pp][:, 0:w],
                                vh_sb[:, hp * 65 * NKC + ch * 65:
                                      hp * 65 * NKC + (ch + 1) * 65],
                                pt_sb[:, cp * 512:cp * 512 + w],
                                start=(ch == 0), stop=(ch == NKC - 1),
                                deps=(vld, ex, pv_evac[pp]), inc=True)
                    pt_use[cp] = mmpv
                ev = p.op("vector", "tensor_copy",
                          ao_sb[:, (h // 2) * TLOC + tqi * 512:
                                (h // 2) * TLOC + tqi * 512 + w]
                          [(h % 2) * 64:(h % 2) * 64 + 64, :],
                          ppv[pp][0:64, 0:w], deps=(mmpv,))
                den_cp = p.op("vector", "tensor_copy",
                              den_st[0:1, hp * TLOC + tqi * 512:
                                     hp * TLOC + tqi * 512 + w],
                              ppv[pp][64:65, 0:w], deps=(rd_dma[hp],))
                pv_evac[pp] = den_cp
            kh_use[hp] = mmqk
            vh_use[hp] = mmpv
            rcp = p.op("vector", "reciprocal",
                       den_rc[0:1, hp * TLOC:(hp + 1) * TLOC],
                       den_st[0:1, hp * TLOC:(hp + 1) * TLOC],
                       deps=(den_cp, rd_dma[hp]))
            rd_dma[hp] = p.dma("gpsimd", rd_dram[h:h + 1, :],
                               den_rc[0:1, hp * TLOC:(hp + 1) * TLOC],
                               deps=(rcp,))
        rdma = rd_dma[0] if rd_dma[0][1] > rd_dma[1][1] else rd_dma[1]
        wld_prev = [None, None]
        evac = [None, None]
        ho_dma = [None, None]
        y1_use = [None, None]
        sctok = None
        rdrep_use = None
        for rt in range(8):
            bc0 = p.dma("sync", rdrep[0:64, :],
                        rd_dram[2 * rt:2 * rt + 1, :].partition_broadcast(64)
                        .squeeze(1), deps=(rdma, rdrep_use))
            bc = p.dma("sync", rdrep[64:128, :],
                       rd_dram[2 * rt + 1:2 * rt + 2, :].partition_broadcast(64)
                       .squeeze(1), deps=(rdma, rdrep_use))
            sc = p.op("vector", "tensor_mul", ao_sb[:, rt * TLOC:(rt + 1) * TLOC],
                      ao_sb[:, rt * TLOC:(rt + 1) * TLOC], rdrep[:],
                      deps=(bc0, bc))
            rdrep_use = sc
        sctok = rdrep_use
        for o in range(8):
            wp = o % 2
            wld = p.dma("sync", wst[:, wp * 1024:(wp + 1) * 1024]
                        .rearrange("p (k c) -> p k c", k=8),
                        woT.rearrange("(k p) e -> p k e", p=128)
                        [:, :, o * 128:(o + 1) * 128],
                        deps=(wld_prev[wp],))
            for ti in range(4):
                w = 512
                pp = ti % 2
                yld = p.dma("sync", y1_ld[:, pp * 512:pp * 512 + w],
                            y1[o * 128:(o + 1) * 128, ti * 512:ti * 512 + w],
                            deps=(y1_use[pp],))
                mm = None
                for k in range(8):
                    mm = p.mm(pwo[pp][:, 0:w],
                              wst[:, wp * 1024 + k * 128:wp * 1024 + (k + 1) * 128],
                              ao_sb[:, k * TLOC + ti * 512:k * TLOC + ti * 512 + w],
                              start=(k == 0), stop=(k == 7),
                              deps=(wld, sctok, evac[pp]), inc=(k == 7))
                if ti == 3:
                    wld_prev[wp] = mm
                ev = p.op("vector", "tensor_add", ho_sb[:, pp * 512:pp * 512 + w],
                          y1_ld[:, pp * 512:pp * 512 + w], pwo[pp][:, 0:w],
                          deps=(mm, yld, ho_dma[pp]))
                evac[pp] = ev
                y1_use[pp] = ev
                ho_dma[pp] = p.dma("gpsimd",
                                   h_o[o * 128:(o + 1) * 128, ti * 512:ti * 512 + w],
                                   ho_sb[:, pp * 512:pp * 512 + w], deps=(ev,))
        p.op("gpsimd", "engine_nop", deps=(p.last_tok["gpsimd"],))

    return build_program(fill)


def build_moe():
    def fill(nc, p, stack):
        r2 = nc.dram_tensor("r2", [D, TLOC], F32, kind="ExternalInput").ap()
        hres = nc.dram_tensor("hres", [D, TLOC], F32, kind="ExternalInput").ap()
        gateT = nc.dram_tensor("gateT", [D, 8], BF16, kind="ExternalInput").ap()
        w1T = nc.dram_tensor("w1T", [8 * D, DIN], BF16, kind="ExternalInput").ap()
        w3T = nc.dram_tensor("w3T", [8 * D, DIN], BF16, kind="ExternalInput").ap()
        w2T = nc.dram_tensor("w2T", [8 * DIN, D], BF16, kind="ExternalInput").ap()
        h2_o = nc.dram_tensor("h2", [D, TLOC], F32, kind="ExternalOutput").ap()
        we_dram = nc.dram_tensor("we_scratch", [8, TLOC], F32, kind="Internal").ap()

        r_ld = sb(nc, stack, "r_ld", [128, 8 * 256], F32)
        r_bf = sb(nc, stack, "r_bf", [128, 8 * TLOC], BF16)
        gt_sb = sb(nc, stack, "gt_sb", [128, 8 * 8], BF16)
        lg_sb = sb(nc, stack, "lg_sb", [128, 128], F32)
        m1_sb = sb(nc, stack, "m1_sb", [128, 16], F32)
        m2_sb = sb(nc, stack, "m2_sb", [128, 16], F32)
        s1_sb = sb(nc, stack, "s1_sb", [128, 16], F32)
        s2_sb = sb(nc, stack, "s2_sb", [128, 16], F32)
        lp_sb = sb(nc, stack, "lp_sb", [128, 128], F32)
        is0_sb = sb(nc, stack, "is0_sb", [128, 128], F32)
        tmp_sb = sb(nc, stack, "tmp_sb", [128, 128], F32)
        is2_sb = sb(nc, stack, "is2_sb", [128, 128], F32)
        we_sb = sb(nc, stack, "we_sb", [128, 128], F32)
        weT_sb = sb(nc, stack, "weT_sb", [8, TLOC], F32)
        ident = sb(nc, stack, "ident", [128, 128], F32)
        werep = sb(nc, stack, "werep", [128, TLOC], F32)
        h1_sb = sb(nc, stack, "h1_sb", [128, 16 * 512], BF16)
        g3_sb = sb(nc, stack, "g3_sb", [128, 16 * 512], BF16)
        he_sb = sb(nc, stack, "he_sb", [128, 16 * 512], BF16)
        acc_sb = sb(nc, stack, "acc_sb", [128, 8 * TLOC], F32)
        hr_ld = sb(nc, stack, "hr_ld", [128, 2 * 512], F32)
        o_sb = sb(nc, stack, "o_sb", [128, 2 * 512], F32)
        wst = sb(nc, stack, "wst", [128, 4 * 2048], BF16)
        pg = ps(nc, stack, "pg", [128, 512])
        pt = ps(nc, stack, "pt", [8, 128])
        pm = [ps(nc, stack, "pm0", [128, 512]), ps(nc, stack, "pm1", [128, 512])]

        cv = None
        for i in range(64):
            ld = p.dma("sync", r_ld[:, (i % 8) * 256:(i % 8) * 256 + 256],
                       r2.rearrange("(o p) t -> p o t", p=128)
                       [:, i // 8, (i % 8) * 256:(i % 8) * 256 + 256],
                       deps=(cv,))
            cv = p.op("vector", "tensor_copy",
                      r_bf[:, (i // 8) * TLOC + (i % 8) * 256:
                           (i // 8) * TLOC + (i % 8) * 256 + 256],
                      r_ld[:, (i % 8) * 256:(i % 8) * 256 + 256], deps=(ld,))
        xtok = cv
        gtok = None
        for k in range(8):
            gtok = p.dma("sync", gt_sb[:, k * 8:(k + 1) * 8],
                         gateT[k * 128:(k + 1) * 128, :])
        t_id = p.op("gpsimd", "memset", ident[:], 1.0)
        t_id = p.op("gpsimd", "affine_select", ident[:], ident[:], [[1, 128]],
                    ALU.is_equal, 0.0, base=0, channel_multiplier=-1, deps=(t_id,))

        lgev = None
        for tc in range(16):
            mm = None
            for k in range(8):
                mm = p.mm(pg[:, 0:8],
                          r_bf[:, k * TLOC + tc * 128:k * TLOC + (tc + 1) * 128],
                          gt_sb[:, k * 8:(k + 1) * 8],
                          start=(k == 0), stop=(k == 7),
                          deps=(xtok, gtok, lgev), inc=(k == 7))
            lgev = p.op("vector", "tensor_copy", lg_sb[:, tc * 8:(tc + 1) * 8],
                        pg[:, 0:8], deps=(mm,))
        lg3 = lg_sb[:].rearrange("p (t e) -> p t e", e=8)
        t0 = p.op("vector", "tensor_reduce", m1_sb[:], lg3, AX.X, ALU.max,
                  deps=(lgev,))
        t1 = p.op("vector", "tensor_tensor",
                  lp_sb[:].rearrange("p (t e) -> p t e", e=8), lg3,
                  m1_sb[:].unsqueeze(2).broadcast_to([128, 16, 8]), ALU.subtract)
        t2 = p.op("vector", "tensor_single_scalar", is0_sb[:], lp_sb[:], 0.0,
                  ALU.is_equal)
        t3 = p.op("vector", "scalar_tensor_tensor", tmp_sb[:], is0_sb[:], -1e30,
                  lp_sb[:], ALU.mult, ALU.add)
        t4 = p.op("vector", "tensor_reduce", m2_sb[:],
                  tmp_sb[:].rearrange("p (t e) -> p t e", e=8), AX.X, ALU.max)
        t5 = p.op("scalar", "activation", s2_sb[:], m2_sb[:], AF.Sigmoid,
                  deps=(t4,))
        t6 = p.op("vector", "tensor_scalar", s1_sb[:], s2_sb[:], -1.0, 1.0,
                  op0=ALU.mult, op1=ALU.add, deps=(t5,))
        t7 = p.op("vector", "tensor_tensor", is2_sb[:].rearrange("p (t e) -> p t e", e=8),
                  tmp_sb[:].rearrange("p (t e) -> p t e", e=8),
                  m2_sb[:].unsqueeze(2).broadcast_to([128, 16, 8]), ALU.is_equal)
        t8 = p.op("vector", "tensor_tensor",
                  we_sb[:].rearrange("p (t e) -> p t e", e=8),
                  is0_sb[:].rearrange("p (t e) -> p t e", e=8),
                  s1_sb[:].unsqueeze(2).broadcast_to([128, 16, 8]), ALU.mult)
        t9 = p.op("vector", "tensor_tensor",
                  is2_sb[:].rearrange("p (t e) -> p t e", e=8),
                  is2_sb[:].rearrange("p (t e) -> p t e", e=8),
                  s2_sb[:].unsqueeze(2).broadcast_to([128, 16, 8]), ALU.mult)
        t10 = p.op("vector", "tensor_add", we_sb[:], we_sb[:], is2_sb[:])
        wetok = None
        for tc in range(16):
            mmt = p.add("tensor",
                        lambda E, tc=tc: E.transpose(
                            pt[:, 0:128], we_sb[:, tc * 8:(tc + 1) * 8], ident[:]),
                        deps=[t10, t_id, wetok], serial=False)
            wetok = p.op("vector", "tensor_copy",
                         weT_sb[0:8, tc * 128:(tc + 1) * 128], pt[:, 0:128],
                         deps=(mmt,))
        wedma = p.dma("gpsimd", we_dram, weT_sb[0:8, :], deps=(wetok,))

        wld_prev = None
        evac = [None, None]
        h2_dma = [None, None]
        hr_use = [None, None]
        werep_use = None
        acc_tok = {}
        h1_read = None
        for e in range(8):
            bc = p.dma("sync", werep[:],
                       we_dram[e:e + 1, :].partition_broadcast(128).squeeze(1),
                       deps=(wedma, werep_use))
            for ti in range(4):
                w = 512
                tcols = slice(ti * 512, ti * 512 + w)
                toks_h1 = []
                for o in range(16):
                    pp = o % 2
                    wld = p.dma("sync", wst[:, pp * 2048:pp * 2048 + 1024]
                                .rearrange("p (k c) -> p k c", k=8),
                                w1T.rearrange("(k p) e2 -> p k e2", p=128)
                                [:, e * 8:(e + 1) * 8, o * 128:(o + 1) * 128],
                                deps=(wld_prev,))
                    mm = None
                    for k in range(8):
                        mm = p.mm(pm[pp][:, 0:w],
                                  wst[:, pp * 2048 + k * 128:pp * 2048 + (k + 1) * 128],
                                  r_bf[:, k * TLOC + ti * 512:k * TLOC + ti * 512 + w],
                                  start=(k == 0), stop=(k == 7),
                                  deps=(wld, xtok, evac[pp]), inc=(k == 7))
                    ev = p.op("scalar", "activation", h1_sb[:, o * 512:o * 512 + w],
                              pm[pp][:, 0:w], AF.Silu, deps=(mm, h1_read))
                    evac[pp] = ev
                    toks_h1.append(ev)
                    wld2 = p.dma("sync", wst[:, pp * 2048 + 1024:pp * 2048 + 2048]
                                 .rearrange("p (k c) -> p k c", k=8),
                                 w3T.rearrange("(k p) e2 -> p k e2", p=128)
                                 [:, e * 8:(e + 1) * 8, o * 128:(o + 1) * 128],
                                 deps=(wld_prev,))
                    mm2 = None
                    for k in range(8):
                        mm2 = p.mm(pm[pp][:, 0:w],
                                   wst[:, pp * 2048 + 1024 + k * 128:
                                       pp * 2048 + 1024 + (k + 1) * 128],
                                   r_bf[:, k * TLOC + ti * 512:k * TLOC + ti * 512 + w],
                                   start=(k == 0), stop=(k == 7),
                                   deps=(wld2, xtok, ev), inc=(k == 7))
                    wld_prev = mm2
                    ev2 = p.op("vector", "tensor_copy", g3_sb[:, o * 512:o * 512 + w],
                               pm[pp][:, 0:w], deps=(mm2, h1_read))
                    evac[pp] = ev2
                    ev3 = p.op("vector", "tensor_mul", he_sb[:, o * 512:o * 512 + w],
                               h1_sb[:, o * 512:o * 512 + w],
                               g3_sb[:, o * 512:o * 512 + w], deps=(ev, ev2, h1_read))
                    ev4 = p.op("vector", "tensor_mul", he_sb[:, o * 512:o * 512 + w],
                               he_sb[:, o * 512:o * 512 + w], werep[:, tcols],
                               deps=(bc,))
                    toks_h1.append(ev4)
                he_tok = toks_h1[-1]
                for o2 in range(8):
                    pp = o2 % 2
                    wld3 = p.dma("sync", wst[:, pp * 2048:pp * 2048 + 16 * 128]
                                 .rearrange("p (k c) -> p k c", k=16),
                                 w2T.rearrange("(k p) d -> p k d", p=128)
                                 [:, e * 16:(e + 1) * 16, o2 * 128:(o2 + 1) * 128],
                                 deps=(wld_prev,))
                    mm3 = None
                    for k in range(16):
                        mm3 = p.mm(pm[pp][:, 0:w],
                                   wst[:, pp * 2048 + k * 128:pp * 2048 + (k + 1) * 128],
                                   he_sb[:, k * 512:k * 512 + w],
                                   start=(k == 0), stop=(k == 15),
                                   deps=(wld3, he_tok, evac[pp]), inc=(k == 15))
                    wld_prev = mm3
                    aslot = acc_sb[:, o2 * TLOC + ti * 512:o2 * TLOC + ti * 512 + w]
                    if e == 0:
                        ev5 = p.op("vector", "tensor_copy", aslot, pm[pp][:, 0:w],
                                   deps=(mm3,))
                    else:
                        ev5 = p.op("vector", "tensor_add", aslot, aslot,
                                   pm[pp][:, 0:w], deps=(mm3, acc_tok.get((o2, ti))))
                    evac[pp] = ev5
                    acc_tok[(o2, ti)] = ev5
                h1_read = mm3
            werep_use = he_tok
        for o2 in range(8):
            for ti in range(4):
                w = 512
                pp = ti % 2
                yld = p.dma("sync", hr_ld[:, pp * 512:pp * 512 + w],
                            hres[o2 * 128:(o2 + 1) * 128, ti * 512:ti * 512 + w],
                            deps=(hr_use[pp],))
                ev = p.op("vector", "tensor_add", o_sb[:, pp * 512:pp * 512 + w],
                          hr_ld[:, pp * 512:pp * 512 + w],
                          acc_sb[:, o2 * TLOC + ti * 512:o2 * TLOC + ti * 512 + w],
                          deps=(yld, acc_tok[(o2, ti)], h2_dma[pp]))
                hr_use[pp] = ev
                h2_dma[pp] = p.dma("gpsimd",
                                   h2_o[o2 * 128:(o2 + 1) * 128, ti * 512:ti * 512 + w],
                                   o_sb[:, pp * 512:pp * 512 + w], deps=(ev,))
        p.op("gpsimd", "engine_nop", deps=(p.last_tok["gpsimd"],))

    return build_program(fill)


def _fm(x_pos):
    return np.ascontiguousarray(x_pos.T)


def _cores_fm(full_fm):
    return [np.ascontiguousarray(full_fm[:, c * TLOC:(c + 1) * TLOC])
            for c in range(NCORES)]


def kernel(**inp):
    inp = {k: np.asarray(v) for k, v in inp.items()}
    x0 = inp["L0_attn_x"].astype(np.float32, copy=False)
    x0_pos = np.ascontiguousarray(x0.reshape(B * S, D))

    arrs = []
    for i in range(NLAYER):
        arrs += [inp["L0_attn_W_in"][i].T, inp["L0_attn_W_out"][i].T,
                 inp["L0_attn_W_x"][i].T, inp["L0_attn_W_dt"][i].T]
    arrs += [inp["L0_ffn_w1_w"].T, inp["L0_ffn_w2_w"].T, inp["L0_ffn_h1_w"].T,
             inp["L0_ffn_h2_w"].T, inp["L0_ffn_w3_w"].T]
    arrs += [inp["L1_attn_W"].T]
    arrs += [inp["L1_ffn_wq"].T, inp["L1_ffn_wk"].T, inp["L1_ffn_wv"].T,
             inp["L1_ffn_wo"].T, inp["L1_ffn_gate_w"].T]
    arrs += [np.concatenate([inp["L1_ffn_w1"][e].T for e in range(8)], 0),
             np.concatenate([inp["L1_ffn_w3"][e].T for e in range(8)], 0),
             np.concatenate([inp["L1_ffn_w2"][e].T for e in range(8)], 0)]
    bfw = prep_weights(arrs)
    wb = {"winT": [], "woutT": [], "wxT": [], "wdtT": []}
    for i in range(NLAYER):
        wb["winT"].append(bfw[4 * i]); wb["woutT"].append(bfw[4 * i + 1])
        wb["wxT"].append(bfw[4 * i + 2]); wb["wdtT"].append(bfw[4 * i + 3])
    (w1T_b, w2T_b, h1T_b, h2T_b, w3T_b, l1wT_b, wqT_b, wkT_b, wvT_b, woT_b,
     gateT_b, mw1T_b, mw3T_b, mw2T_b) = bfw[32:]

    raw = {"conv_w": inp["L0_attn_conv_w"], "conv_b": inp["L0_attn_conv_b"],
           "b_dt": inp["L0_attn_b_dt"], "A_log": inp["L0_attn_A_log"],
           "D_par": inp["L0_attn_D_par"]}

    u0_pos = run_norm("ln", x0_pos)
    u8_fm = run_mamba_stack(_fm(u0_pos), wb, raw)
    x1_pos = run_norm("resid_ln", _fm(u8_fm).copy() if False else
                      np.ascontiguousarray(u8_fm.T), r_pos_full=x0_pos)

    v_pos = run_norm("ln", x1_pos)
    v_fm = _fm(v_pos)
    maps = []
    for c in range(NCORES):
        maps.append(dict(v=np.ascontiguousarray(v_fm[:, c * TLOC:(c + 1) * TLOC]),
                         w1T=w1T_b, w2T=w2T_b, h1T=h1T_b, h2T=h2T_b, w3T=w3T_b,
                         s1=inp["L0_ffn_w1_s"].reshape(-1, 1),
                         s2=inp["L0_ffn_w2_s"].reshape(-1, 1),
                         sh1=inp["L0_ffn_h1_s"].reshape(-1, 1),
                         sh2=inp["L0_ffn_h2_s"].reshape(-1, 1),
                         s3=inp["L0_ffn_w3_s"].reshape(-1, 1)))
    res = _run(get_prog("l0ffn", build_l0ffn), maps)
    f_fm = np.concatenate([np.asarray(res[c]["f"]) for c in range(NCORES)], 1)
    x2_pos = run_norm("resid_ln", np.ascontiguousarray(f_fm.T), r_pos_full=x1_pos)

    w_pos = run_norm("ln", x2_pos)
    w_fm = _fm(w_pos)
    maps = [dict(v=np.ascontiguousarray(w_fm[:, c * TLOC:(c + 1) * TLOC]),
                 wT=l1wT_b, bias=inp["L1_attn_bias"].reshape(-1, 1))
            for c in range(NCORES)]
    res = _run(get_prog("l1attn", build_l1attn), maps)
    g_fm = np.concatenate([np.asarray(res[c]["g"]) for c in range(NCORES)], 1)
    x3_pos = run_norm("resid_ln", np.ascontiguousarray(g_fm.T), r_pos_full=x2_pos)

    y1_pos = run_norm("ln", x3_pos)
    r1_pos = run_norm("rms", y1_pos, w=inp["L1_ffn_ln1_w"])
    r1_fm = _fm(r1_pos)
    maps = [dict(x=np.ascontiguousarray(r1_fm[:, c * TLOC:(c + 1) * TLOC]),
                 wqT=wqT_b, wkT=wkT_b, wvT=wvT_b) for c in range(NCORES)]
    res = _run(get_prog("qkv", build_qkv), maps)
    ks = [np.asarray(res[c]["k"]) for c in range(NCORES)]
    vs = [np.asarray(res[c]["vpos"]) for c in range(NCORES)]
    qs = [np.asarray(res[c]["q"]) for c in range(NCORES)]
    kfull = [np.concatenate(ks[4 * b:4 * b + 4], 1) for b in range(B)]
    vfull = [np.concatenate(vs[4 * b:4 * b + 4], 0) for b in range(B)]
    y1_fm = _fm(y1_pos)
    maps = [dict(q=qs[c], kfull=kfull[c // 4], vfull=vfull[c // 4], woT=woT_b,
                 y1=np.ascontiguousarray(y1_fm[:, c * TLOC:(c + 1) * TLOC]))
            for c in range(NCORES)]
    res = _run(get_prog("attn", build_attn), maps)
    h_fm = np.concatenate([np.asarray(res[c]["h"]) for c in range(NCORES)], 1)
    h_pos = np.ascontiguousarray(h_fm.T)
    r2_pos = run_norm("rms", h_pos, w=inp["L1_ffn_ln2_w"])
    r2_fm = _fm(r2_pos)
    maps = [dict(r2=np.ascontiguousarray(r2_fm[:, c * TLOC:(c + 1) * TLOC]),
                 hres=np.ascontiguousarray(h_fm[:, c * TLOC:(c + 1) * TLOC]),
                 gateT=gateT_b, w1T=mw1T_b, w3T=mw3T_b, w2T=mw2T_b)
            for c in range(NCORES)]
    res = _run(get_prog("moe", build_moe), maps)
    h2_fm = np.concatenate([np.asarray(res[c]["h2"]) for c in range(NCORES)], 1)
    x4_pos = run_norm("resid_ln", np.ascontiguousarray(h2_fm.T), r_pos_full=x3_pos)
    return np.ascontiguousarray(x4_pos.reshape(B, S, D)).astype(np.float32)


GRP = [[0, 1, 2, 3], [4, 5, 6, 7]]
CSH = 512
MSC = 16
SSC = 8
SCHm = 1024


class Arena:
    def __init__(self, tile, nelem):
        self.tile, self.n, self.off = tile, nelem, 0

    def bf(self, nelem):
        assert self.off + nelem <= self.n, (self.off, nelem)
        v = self.tile[:, self.off:self.off + nelem]
        self.off += nelem
        return v

    def f32(self, nelem):
        self.off += self.off % 2
        assert self.off + 2 * nelem <= self.n, (self.off, nelem)
        v = self.tile[:, self.off:self.off + 2 * nelem].bitcast(F32)
        self.off += 2 * nelem
        return v

    def mark(self):
        return self.off

    def reset(self, mark):
        self.off = mark


MEGA_NL = [NLAYER]
MEGA_STOP = [None]


def build_mega():
    NL = MEGA_NL[0]
    STOP = MEGA_STOP[0]

    def fill(nc, p, stack):
        x0q = nc.dram_tensor("x0q", [D, TLOC], F32, kind="ExternalInput").ap()
        out_o = nc.dram_tensor("out", [D, TLOC], F32, kind="ExternalOutput").ap()
        ew = {}

        def EIN(name, shape):
            ew[name] = nc.dram_tensor(name, shape, F32, kind="ExternalInput").ap()
        for i in range(NLAYER):
            EIN(f"winT{i}", [D, 2 * CSH]); EIN(f"woutT{i}", [DIN, D])
            EIN(f"wxT{i}", [CSH, 96]); EIN(f"wdtT{i}", [DTR, CSH])
            EIN(f"convw{i}", [CSH, 4]); EIN(f"convb{i}", [CSH, 1])
            EIN(f"bdt{i}", [CSH, 1]); EIN(f"alog{i}", [CSH, NST])
            EIN(f"dpar{i}", [CSH, 1])
        EIN("fw1T", [D, 4096]); EIN("fw2T", [D, 4096])
        EIN("fh1T", [4096, 4096]); EIN("fh2T", [4096, 4096])
        EIN("fw3T", [4096, D])
        EIN("fs1", [4096, 1]); EIN("fs2", [4096, 1]); EIN("fsh1", [4096, 1])
        EIN("fsh2", [4096, 1]); EIN("fs3", [D, 1])
        EIN("l1wT", [D, D]); EIN("l1b", [D, 1])
        EIN("ln1w", [1, D]); EIN("ln2w", [1, D])
        EIN("wqT", [D, D]); EIN("wkT", [D, D]); EIN("wvT", [D, D]); EIN("woT", [D, D])
        EIN("gateT", [D, 8])
        EIN("mw1T", [8 * D, DIN]); EIN("mw3T", [8 * D, DIN]); EIN("mw2T", [8 * DIN, D])

        def DRT(n, s, d):
            return nc.dram_tensor(n, s, d, kind="Internal").ap()
        u_con = DRT("u_con", [D, TLOC], F32)
        ub_con = DRT("ub_con", [D, TLOC], BF16)
        u_ag = DRT("u_ag", [4 * D, TLOC], F32)
        ub_ag = DRT("ub_ag", [4 * D, TLOC], BF16)
        xd_con = DRT("xd_con", [96, S], F32)
        xd_ar = DRT("xd_ar", [96, S], F32)
        yg_con = DRT("yg_con", [CSH, S], BF16)
        yg_ag = DRT("yg_ag", [DIN, S], BF16)
        xin_dr = DRT("xin_dr", [CSH, S], BF16)
        g_dr = DRT("g_dr", [CSH, S], BF16)
        dl_dr = DRT("dl_dr", [CSH, S], BF16)
        du_dr = DRT("du_dr", [CSH, S], BF16)
        b_dr = DRT("b_dr", [NST, S], BF16)
        cpos_dr = DRT("cpos_dr", [1, S * NST], BF16)
        ns_dr = DRT("ns_dr", [2, 512], F32)
        u8_dr = DRT("u8_dr", [D, TLOC], F32)
        x1_dr = DRT("x1_dr", [D, TLOC], F32)
        x2_dr = DRT("x2_dr", [D, TLOC], F32)
        v2f_dr = DRT("v2f_dr", [D, TLOC], F32)
        v2b_dr = DRT("v2b_dr", [D, TLOC], BF16)
        f_dr = DRT("f_dr", [D, TLOC], F32)
        x3_dr = DRT("x3_dr", [D, TLOC], F32)
        y1_dr = DRT("y1_dr", [D, TLOC], F32)
        h4_dr = DRT("h4_dr", [D, TLOC], F32)
        q_dr = DRT("q_dr", [D, TLOC], BF16)
        k_con = DRT("k_con", [D, TLOC], BF16)
        v_con = DRT("v_con", [TLOC, D], BF16)
        k_ag = DRT("k_ag", [4 * D, TLOC], BF16)
        v_ag = DRT("v_ag", [4 * TLOC, D], BF16)
        rd_dr = DRT("rd_dr", [16, TLOC], F32)
        we_dr = DRT("we_dr", [8, TLOC], F32)
        h1c_dr = DRT("h1c_dr", [4096, 4096], BF16)
        h2c_dr = DRT("h2c_dr", [4096, 4096], BF16)
        m1c_dr = DRT("m1c_dr", [8 * D, DIN], BF16)
        m3c_dr = DRT("m3c_dr", [8 * D, DIN], BF16)
        m2c_dr = DRT("m2c_dr", [8 * DIN, D], BF16)

        AR_N = 92160
        arena_t = sb(nc, stack, "arena", [128, AR_N], BF16)
        A = Arena(arena_t, AR_N)
        ones_f = sb(nc, stack, "ones_f", [128, 1], F32)
        ones_b = sb(nc, stack, "ones_b", [128, 128], BF16)
        triu_b = sb(nc, stack, "triu_b", [128, 128], BF16)
        ident = sb(nc, stack, "ident", [128, 128], F32)
        stg = sb(nc, stack, "stg", [128, 1024], F32)
        stgc = sb(nc, stack, "stgc", [128, 1024], F32)
        stgc2 = sb(nc, stack, "stgc2", [128, 1024], BF16)
        nsc = sb(nc, stack, "nsc", [128, 512], F32)
        nrs = sb(nc, stack, "nrs", [128, 512], F32)
        nrb = sb(nc, stack, "nrb", [128, 512], BF16)
        csc = sb(nc, stack, "csc", [128, 512], F32)
        st_mt = sb(nc, stack, "st_mt", [128, 512], F32)
        st_vt = sb(nc, stack, "st_vt", [128, 512], F32)
        st_m = st_mt[0:1, :]
        st_v = st_vt[0:1, :]
        mrep = sb(nc, stack, "mrep", [128, 512], F32)
        rrep = sb(nc, stack, "rrep", [128, 512], F32)
        mb40 = sb(nc, stack, "mb40", [128, 1], F32)
        wcl = sb(nc, stack, "wcl", [128, 160], F32)
        psA = [ps(nc, stack, "psA0", [128, 512]), ps(nc, stack, "psA1", [128, 512])]
        psB = [ps(nc, stack, "psB0", [128, 512]), ps(nc, stack, "psB1", [128, 512])]
        psN = ps(nc, stack, "psN", [96, 512])
        ps65 = [ps(nc, stack, "ps650", [65, 512]), ps(nc, stack, "ps651", [65, 512])]
        psX = stack.enter_context(nc.psum_tensor("psX", [128, 256], BF16))
        psS = [psN[0:1, 0:512], ps65[0][0:1, 0:512]]
        psT4 = psX[0:8, 0:256].bitcast(F32)
        psTb = psX[:, 0:16]

        t_ones = p.op("gpsimd", "memset", ones_f[:], 1.0)
        t_onesb = p.op("gpsimd", "memset", ones_b[:], 1.0)
        t_tri = p.op("gpsimd", "memset", triu_b[:], 1.0)
        t_tri = p.op("gpsimd", "affine_select", triu_b[:], triu_b[:], [[1, 128]],
                     ALU.is_ge, 0.0, base=0, channel_multiplier=-1, deps=(t_tri,))
        t_id = p.op("gpsimd", "memset", ident[:], 1.0)
        t_id = p.op("gpsimd", "affine_select", ident[:], ident[:], [[1, 128]],
                    ALU.is_equal, 0.0, base=0, channel_multiplier=-1, deps=(t_id,))
        t_m40 = p.op("gpsimd", "memset", mb40[:], -40.0)

        def barrier():
            return tuple(t for t in (p.last_tok[e] for e in ENGS) if t)

        def collective(kind, op_, ins, outs, deps=()):
            return p.add("gpsimd", lambda E: E.collective_compute(
                kind, op_, replica_groups=GRP,
                ins=[a.opt() for a in ins], outs=[a.opt() for a in outs]),
                deps=tuple(deps) + barrier())

        def cast_jobs():
            for (nm, dst, rows) in (("fh1T", h1c_dr, 4096), ("fh2T", h2c_dr, 4096),
                                    ("mw1T", m1c_dr, 8 * D), ("mw3T", m3c_dr, 8 * D),
                                    ("mw2T", m2c_dr, 8 * DIN)):
                src = ew[nm]
                for r0 in range(0, rows, 128):
                    for c0 in range(0, src.shape[1], 1024):
                        yield (src[r0:r0 + 128, c0:c0 + 1024],
                               dst[r0:r0 + 128, c0:c0 + 1024])
        _cj = iter(cast_jobs())
        cst = {"cp": None, "st": None, "done": False}

        def drain_casts(n):
            if cst["done"]:
                return
            for _ in range(n):
                try:
                    sr, dsl = next(_cj)
                except StopIteration:
                    cst["done"] = True
                    return
                ld = p.dma("scalar", stgc[:], sr, deps=(cst["cp"],))
                cp = p.op("gpsimd", "tensor_copy", stgc2[:], stgc[:],
                          deps=(ld, cst["st"]))
                cst["cp"] = cp
                cst["st"] = p.dma("gpsimd", dsl, stgc2[:], deps=(cp,))

        nwar = {"sq": None, "s1": None, "s2": None, "ap": None, "ns": None,
                "st": [(), ()], "st2": [None, None]}

        def fm_norm(dst_dr, src_dr, res_dr=None, bf_dr=None, wcol=None,
                    kind="ln", dep0=()):
            out_tok = None
            for tt in range(4):
                w = 512
                mm1 = mm2 = None
                for o in range(8):
                    pp = o % 2
                    stv = stg[:, pp * 512:(pp + 1) * 512]
                    lds = p.dma("sync", stv,
                                src_dr[o * 128:(o + 1) * 128,
                                       tt * 512:tt * 512 + w],
                                deps=tuple(nwar["st"][pp] or ())
                                + (wst_state["war"],) + tuple(dep0))
                    if kind == "ln":
                        mm1 = p.mm(psS[0][:, 0:w], ones_f[:], stv,
                                   start=(o == 0), stop=(o == 7),
                                   deps=(t_ones, nwar["s1"], lds), inc=True)
                    sqv = p.op("scalar", "activation", nsc[:, 0:w], stv,
                               AF.Square,
                               deps=(nwar["sq"], lds, nwar["s2"], nwar["ns"],
                                     nwar["st2"][1]) if o == 0 else
                                    (nwar["sq"], lds))
                    nwar["st"][pp] = (sqv,) if kind != "ln" else (mm1, sqv)
                    mm2 = p.mm(psS[1][:, 0:w], ones_f[:], nsc[:, 0:w],
                               start=(o == 0), stop=(o == 7),
                               deps=(t_ones, sqv, nwar["s2"]), inc=True)
                    nwar["sq"] = mm2
                if kind == "ln":
                    p.op("vector", "tensor_scalar_mul", st_m[:, 0:w],
                         psS[0][:, 0:w], 1.0 / D, deps=(mm1, nwar["ns"]))
                    p.op("vector", "tensor_scalar_mul", st_v[:, 0:w],
                         psS[1][:, 0:w], 1.0 / D, deps=(mm2,))
                    p.op("vector", "tensor_mul", nsc[0:1, 0:w],
                         st_m[:, 0:w], st_m[:, 0:w])
                    p.op("vector", "tensor_sub", st_v[:, 0:w],
                         st_v[:, 0:w], nsc[0:1, 0:w])
                    s4 = p.op("vector", "tensor_scalar", st_v[:, 0:w],
                              st_v[:, 0:w], 1.0, EPS, op0=ALU.mult, op1=ALU.add)
                else:
                    s4 = p.op("vector", "tensor_scalar", st_v[:, 0:w],
                              psS[1][:, 0:w], 1.0 / D, EPS, op0=ALU.mult,
                              op1=ALU.add, deps=(mm2, nwar["ns"]))
                    s4 = p.op("vector", "memset", st_m[:, 0:w], 0.0)
                l1 = p.op("scalar", "activation", st_v[:, 0:w], st_v[:, 0:w],
                          AF.Ln, deps=(s4,))
                l1 = p.op("scalar", "activation", st_v[:, 0:w], st_v[:, 0:w],
                          AF.Exp, scale=-0.5)
                d0 = p.dma("gpsimd", ns_dr[0:1, 0:w], st_m[:, 0:w],
                           deps=(l1, nwar["ap"]))
                d0 = p.dma("gpsimd", ns_dr[1:2, 0:w], st_v[:, 0:w],
                           deps=(l1,))
                nwar["ns"] = d0
                b0 = p.dma("sync", mrep[:, 0:w],
                           ns_dr[0:1, 0:w].partition_broadcast(128).squeeze(1),
                           deps=(d0, nwar["ap"]))
                b1 = p.dma("sync", rrep[:, 0:w],
                           ns_dr[1:2, 0:w].partition_broadcast(128).squeeze(1),
                           deps=(d0, nwar["ap"]))
                for o in range(8):
                    pp = o % 2
                    stv = stg[:, pp * 512:(pp + 1) * 512]
                    rsl = slice(o * 128, (o + 1) * 128)
                    csl = slice(tt * 512, tt * 512 + w)
                    lds = p.dma("sync", stv, src_dr[rsl, csl],
                                deps=tuple(nwar["st"][pp] or ()))
                    p.op("vector", "tensor_sub", nsc[:, 0:w], stv,
                         mrep[:, 0:w], deps=(b0, lds, nwar["sq"], nwar["ns"]))
                    apv = p.op("vector", "tensor_mul", nsc[:, 0:w],
                               nsc[:, 0:w], rrep[:, 0:w], deps=(b1,))
                    nwar["st"][pp] = (apv,)
                    if wcol is not None:
                        apv = p.op("vector", "tensor_scalar", nsc[:, 0:w],
                                   nsc[:, 0:w], wcol[:, o:o + 1], None,
                                   op0=ALU.mult)
                    if res_dr is not None:
                        ldr = p.dma("sync", nrs[:, 0:512], res_dr[rsl, csl],
                                    deps=(nwar["st2"][0],))
                        apv = p.op("vector", "tensor_add", nsc[:, 0:w],
                                   nsc[:, 0:w], nrs[:, 0:512], deps=(ldr,))
                        nwar["st2"][0] = apv
                    ot = p.dma("gpsimd", dst_dr[rsl, csl], nsc[:, 0:w],
                               deps=(apv,))
                    nwar["ns"] = ot
                    if bf_dr is not None:
                        cvb = p.op("vector", "tensor_copy",
                                   nrb[:, 0:512], nsc[:, 0:w],
                                   deps=(nwar["st2"][1],))
                        ot = p.dma("gpsimd", bf_dr[rsl, csl], nrb[:, 0:512],
                                   deps=(cvb,))
                        nwar["st2"][1] = ot
                    out_tok = ot
                    nwar["ap"] = apv
                    nwar["s1"] = apv
                    nwar["s2"] = apv
            wst_state["war"] = nwar["s2"]
            return out_tok

        def spill(dram, view, deps=(), eng="gpsimd"):
            tk = None
            for o in range(8):
                tk = p.dma(eng, dram[o * 128:(o + 1) * 128, :],
                           view[:, o * TLOC:(o + 1) * TLOC], deps=deps)
            return tk

        def fill_s(view, dram, deps=()):
            tk = None
            for o in range(8):
                tk = p.dma("sync", view[:, o * TLOC:(o + 1) * TLOC],
                           dram[o * 128:(o + 1) * 128, :], deps=deps)
            return tk

        wst_state = {"war": None}

        def ldcast(dst_bf, src_f32, deps=()):
            rows = src_f32.shape[0]
            cols = src_f32.shape[1]
            ld = p.dma("sync", stg[0:rows, 0:cols], src_f32,
                       deps=(wst_state["war"],) + tuple(deps))
            tk = p.op("vector", "tensor_copy", dst_bf[0:rows, 0:cols],
                      stg[0:rows, 0:cols], deps=(ld,))
            wst_state["war"] = tk
            return tk

        identb = sb(nc, stack, "identb", [128, 128], BF16)
        t_idb = p.op("gpsimd", "memset", identb[:], 1.0)
        t_idb = p.op("gpsimd", "affine_select", identb[:], identb[:], [[1, 128]],
                     ALU.is_equal, 0.0, base=0, channel_multiplier=-1,
                     deps=(t_idb,))
        ixq = nc.dram_tensor("ixq", [128, 1], mybir.dt.int32,
                             kind="ExternalInput").ap()
        ixq_sb = sb(nc, stack, "ixq_sb", [128, 1], mybir.dt.int32)
        t_ix = p.dma("sync", ixq_sb[:], ixq)
        yg_seg = None

        t0n = fm_norm(u_con, x0q, bf_dr=ub_con)
        agU = collective("AllGather", ALU.bypass, [u_con], [u_ag], deps=(t0n,))
        agU = collective("AllGather", ALU.bypass, [ub_con], [ub_ag])

        for li in range(NL):
            m0 = A.mark()
            winb = A.bf(8 * 1024)
            wxb = A.bf(4 * 96)
            wdtb = A.bf(512)
            cwf = A.f32(16)
            cbf = A.f32(4)
            bdtf = A.f32(4)
            urhs = A.bf(2 * 8 * 512)
            ring = A.f32(4 * 516)
            xin_c = A.bf(4 * 512)
            gz_c = A.bf(2 * 512)
            dtr_c = A.bf(2 * 512)
            bB_c = A.bf(2 * 512)
            bC_c = A.bf(2 * 512)
            ct_c = A.bf(2 * 64)
            dl_c = A.f32(2 * 512)
            dlb_c = A.bf(2 * 512)
            du_c = A.bf(2 * 512)
            xrl = A.bf(2 * 512)

            wt1 = None
            for k in range(8):
                wt1 = ldcast(winb[:, k * 1024:(k + 1) * 1024],
                             ew[f"winT{li}"][k * 128:(k + 1) * 128, :])
            for ct in range(4):
                wt1 = ldcast(wxb[:, ct * 96:(ct + 1) * 96],
                             ew[f"wxT{li}"][ct * 128:(ct + 1) * 128, :])
            wt1 = ldcast(wdtb, ew[f"wdtT{li}"])
            ldc = None
            for ct in range(4):
                csl = slice(ct * 128, (ct + 1) * 128)
                p.dma("sync", cwf[:, 4 * ct:4 * ct + 4], ew[f"convw{li}"][csl, :])
                p.dma("sync", cbf[:, ct:ct + 1], ew[f"convb{li}"][csl, :])
                ldc = p.dma("sync", bdtf[:, ct:ct + 1], ew[f"bdt{li}"][csl, :])
            rtok = p.op("gpsimd", "memset", ring[:], 0.0, deps=barrier())

            evA = [None, None]
            evN = None
            mmn = None
            xin_sp = [None] * 4
            g_sp = None
            xd_sp = None
            xs_prev = None
            ub_rd = [None, None]
            for sc in range(MSC):
                pp = sc % 2
                r, lc = sc // 4, (sc % 4) * 512
                urv = urhs[:, pp * 4096:(pp + 1) * 4096]
                ldu = None
                for k in range(8):
                    ldu = p.dma("sync", urv[:, k * 512:(k + 1) * 512],
                                ub_ag[r * D + k * 128:r * D + (k + 1) * 128,
                                      lc:lc + 512], deps=(ub_rd[pp], agU))
                conv_toks = []
                mm = None
                for o in range(8):
                    psum = psA[o % 2]
                    for k in range(8):
                        mm = p.mm(psum[:, 0:512],
                                  winb[:, k * 1024 + o * 128:k * 1024 + (o + 1) * 128],
                                  urv[:, k * 512:(k + 1) * 512],
                                  start=(k == 0), stop=(k == 7),
                                  deps=(wt1, ldu, evA[o % 2], rtok), inc=(k == 7))
                    if o < 4:
                        ev = p.op("vector", "tensor_copy",
                                  ring[:, o * 516 + 3:o * 516 + 515],
                                  psum[:, 0:512], deps=(mm,))
                        evA[o % 2] = ev
                        p.op("vector", "tensor_scalar", csc[:, 0:512],
                             ring[:, o * 516:o * 516 + 512],
                             cwf[:, 4 * o:4 * o + 1], None, op0=ALU.mult,
                             deps=(ldc, xs_prev, xd_sp))
                        t3 = None
                        for j in range(1, 4):
                            t3 = p.op("vector", "scalar_tensor_tensor",
                                      csc[:, 0:512],
                                      ring[:, o * 516 + j:o * 516 + j + 512],
                                      cwf[:, 4 * o + j:4 * o + j + 1],
                                      csc[:, 0:512], ALU.mult, ALU.add)
                        xs = p.op("scalar", "activation",
                                  xin_c[:, o * 512:(o + 1) * 512], csc[:, 0:512],
                                  AF.Silu, bias=cbf[:, o:o + 1],
                                  deps=(t3, xin_sp[o], mmn))
                        xs_prev = xs
                        conv_toks.append(xs)
                        xin_sp[o] = p.dma("gpsimd",
                                          xin_dr[o * 128:(o + 1) * 128,
                                                 sc * 512:(sc + 1) * 512],
                                          xin_c[:, o * 512:(o + 1) * 512],
                                          deps=(xs,))
                        p.op("vector", "tensor_copy", ring[:, o * 516:o * 516 + 3],
                             ring[:, o * 516 + 512:o * 516 + 515])
                    else:
                        zz, zp = o - 4, (o - 4) % 2
                        ev = p.op("scalar", "activation",
                                  gz_c[:, zp * 512:(zp + 1) * 512], psum[:, 0:512],
                                  AF.Silu, deps=(mm, g_sp))
                        evA[o % 2] = ev
                        g_sp = p.dma("gpsimd",
                                     g_dr[zz * 128:(zz + 1) * 128,
                                          sc * 512:(sc + 1) * 512],
                                     gz_c[:, zp * 512:(zp + 1) * 512], deps=(ev,))
                for ct in range(4):
                    mmn = p.mm(psN[:, 0:512], wxb[:, ct * 96:(ct + 1) * 96],
                               xin_c[:, ct * 512:(ct + 1) * 512],
                               start=(ct == 0), stop=(ct == 3),
                               deps=(conv_toks[ct], evN, wt1, rtok),
                               inc=(ct == 3))
                evN = p.op("vector", "tensor_copy", csc[0:96, 0:512],
                           psN[:, 0:512], deps=(mmn, xd_sp, xs_prev))
                xd_sp = p.dma("gpsimd", xd_con[:, sc * 512:(sc + 1) * 512],
                              csc[0:96, 0:512], deps=(evN,))
                ub_rd[pp] = mm
            arX = collective("AllReduce", ALU.add, [xd_con], [xd_ar])
            drain_casts(20)

            dl_sp = du_sp = bB_sp = ct_sp = None
            evC = [None, None]
            ld_war = None
            for sc in range(MSC):
                pp = sc % 2
                ssl = slice(sc * 512, (sc + 1) * 512)
                csl2 = slice(pp * 512, (pp + 1) * 512)
                ldt = p.dma("sync", stg[0:64, 0:512], xd_ar[0:64, ssl],
                            deps=(arX, ld_war))
                ldB = p.dma("sync", stg[64:80, 0:512], xd_ar[64:80, ssl])
                ldC = p.dma("sync", stg[64:80, 512:1024], xd_ar[80:96, ssl])
                cdt = p.op("vector", "tensor_copy", dtr_c[0:DTR, csl2],
                           stg[0:64, 0:512], deps=(ldt,))
                cB = p.op("vector", "tensor_copy", bB_c[0:16, csl2],
                          stg[64:80, 0:512], deps=(ldB, bB_sp))
                bB_sp = p.dma("gpsimd", b_dr[:, ssl], bB_c[0:16, csl2],
                              deps=(cB,))
                cC = p.op("vector", "tensor_copy", bC_c[0:16, csl2],
                          stg[64:80, 512:1024], deps=(ldC, ct_sp))
                ld_war = cC
                for j in range(4):
                    mmt = p.add("tensor", lambda E, pp=pp, j=j: E.transpose(
                        psTb[:],
                        bC_c[0:16, pp * 512 + j * 128:pp * 512 + (j + 1) * 128],
                        identb[0:16, 0:16]),
                        deps=[cC, t_idb, ct_sp], serial=False)
                    cte = p.op("vector", "tensor_copy",
                               ct_c[:, pp * 64 + j * 16:pp * 64 + (j + 1) * 16],
                               psTb[:], deps=(mmt,))
                    ct_sp = p.dma("gpsimd",
                                  cpos_dr[0:1, (sc * 512 + j * 128) * NST:
                                          (sc * 512 + (j + 1) * 128) * NST]
                                  .rearrange("one (t n) -> (one t) n", n=NST),
                                  ct_c[:, pp * 64 + j * 16:pp * 64 + (j + 1) * 16],
                                  deps=(cte,))
                for ct in range(4):
                    psum = psB[ct % 2]
                    mmE = p.mm(psum[:, 0:512], wdtb[0:DTR, ct * 128:(ct + 1) * 128],
                               dtr_c[0:DTR, csl2], start=True, stop=True,
                               deps=(cdt, wt1, evC[ct % 2]), inc=True)
                    s0 = p.op("scalar", "activation", csc[:, 0:512],
                              psum[:, 0:512], AF.Abs, bias=bdtf[:, ct:ct + 1],
                              deps=(mmE, ldc, p.last_tok["vector"]))
                    p.op("scalar", "activation", csc[:, 0:512], csc[:, 0:512],
                         AF.Exp, scale=-1.0)
                    p.op("scalar", "activation", dl_c[:, csl2], csc[:, 0:512],
                         AF.Ln, bias=1.0, deps=(dl_sp, p.last_tok["vector"]))
                    s3 = p.op("scalar", "activation", csc[:, 0:512],
                              psum[:, 0:512], AF.Relu, bias=bdtf[:, ct:ct + 1])
                    v0 = p.op("vector", "tensor_add", dl_c[:, csl2],
                              csc[:, 0:512], dl_c[:, csl2], deps=(s3,))
                    evC[ct % 2] = v0
                    cvb = p.op("vector", "tensor_copy", dlb_c[:, csl2],
                               dl_c[:, csl2], deps=(dl_sp,))
                    dl_sp = p.dma("gpsimd", dl_dr[ct * 128:(ct + 1) * 128, ssl],
                                  dlb_c[:, csl2], deps=(cvb,))
                    xr = p.dma("sync", xrl[:, csl2],
                               xin_dr[ct * 128:(ct + 1) * 128, ssl],
                               deps=(xin_sp[ct], du_sp))
                    v1 = p.op("vector", "tensor_mul", du_c[:, csl2],
                              dl_c[:, csl2], xrl[:, csl2], deps=(xr, du_sp))
                    du_sp = p.dma("gpsimd", du_dr[ct * 128:(ct + 1) * 128, ssl],
                                  du_c[:, csl2], deps=(v1,))
            A.reset(m0)
            drain_casts(20)

            m1 = A.mark()
            d_sb = A.bf(2 * SCHm)
            du_sb = A.bf(2 * SCHm)
            xin_sb = A.bf(2 * SCHm)
            g_sb = A.bf(2 * SCHm)
            a_sb = A.f32(2 * SCHm)
            b_sb = A.bf(2 * SCHm)
            H = A.bf(NST * SCHm)
            brep = A.bf(NST * SCHm)
            crep = A.bf(NST * SCHm)
            y1t = A.bf(8 * SCHm)
            y1b = A.bf(8 * SCHm)
            y2t = A.bf(4 * SCHm)
            y2b = A.bf(4 * SCHm)
            ygout = A.bf(2 * SCHm)
            al_sb = A.f32(4 * NST)
            A_sb = A.f32(4 * NST)
            dp_sb = A.f32(4)
            state = A.f32(4 * NST)
            y3v = y1t.bitcast(F32)
            y5v = y1b.bitcast(F32)
            y4v = y2t.bitcast(F32)
            yyv = y2b.bitcast(F32)

            bar0 = barrier()
            t_al = None
            for ct in range(4):
                t_al = p.dma("sync", al_sb[:, ct * NST:(ct + 1) * NST],
                             ew[f"alog{li}"][ct * 128:(ct + 1) * 128, :],
                             deps=bar0)
                t_al = p.dma("sync", dp_sb[:, ct:ct + 1],
                             ew[f"dpar{li}"][ct * 128:(ct + 1) * 128, :])
            t_A = p.op("scalar", "activation", A_sb[:], al_sb[:], AF.Exp,
                       deps=(t_al,) + bar0)
            t_A = p.op("vector", "tensor_scalar_mul", A_sb[:], A_sb[:], -1.0,
                       deps=(t_A,) + bar0)

            def v3(t, n):
                return t.rearrange("p (a b) -> p a b", b=n)

            ld_use = [None, None]
            brep_use = None
            crep_use = None
            yg_sp = [None, None]
            st_tok = [None] * 4
            last_tail = None
            m2tok = None
            for ct in range(4):
                st_tok[ct] = p.op("gpsimd", "memset",
                                  state[:, ct * NST:(ct + 1) * NST], 0.0,
                                  deps=bar0)
            for tch in range(SSC):
                t0_ = tch * SCHm
                tsl = slice(t0_, t0_ + SCHm)
                btoks = []
                for n in range(NST):
                    btoks.append(p.dma(
                        "sync", brep[:, n * SCHm:(n + 1) * SCHm],
                        b_dr[n:n + 1, tsl].partition_broadcast(128).squeeze(1),
                        deps=(brep_use, bB_sp)))
                ctok = p.dma(
                    "sync", crep[:],
                    cpos_dr[0:1, t0_ * NST:(t0_ + SCHm) * NST]
                    .partition_broadcast(128).squeeze(1),
                    deps=(crep_use, ct_sp))
                for ct in range(4):
                    pp = ct % 2
                    cs = slice(ct * 128, (ct + 1) * 128)
                    csl = slice(pp * SCHm, (pp + 1) * SCHm)
                    ld_d = p.dma("sync", d_sb[:, csl], dl_dr[cs, tsl],
                                 deps=(ld_use[pp], dl_sp))
                    ld_du = p.dma("sync", du_sb[:, csl], du_dr[cs, tsl],
                                  deps=(ld_use[pp], du_sp))
                    ld_xin = p.dma("sync", xin_sb[:, csl], xin_dr[cs, tsl],
                                   deps=(ld_use[pp],))
                    ld_g = p.dma("sync", g_sb[:, csl], g_dr[cs, tsl],
                                 deps=(ld_use[pp], g_sp))
                    tsts = []
                    for n in range(NST):
                        np_ = n % 2
                        asl_ = slice(np_ * SCHm, (np_ + 1) * SCHm)
                        awar = tsts[n - 2] if n >= 2 else last_tail
                        ta = p.op("scalar", "activation", a_sb[:, asl_],
                                  d_sb[:, csl], AF.Exp,
                                  scale=A_sb[:, ct * NST + n:ct * NST + n + 1],
                                  deps=(ld_d, t_A, awar))
                        tb = p.op("vector", "tensor_mul", b_sb[:, asl_],
                                  du_sb[:, csl], brep[:, n * SCHm:(n + 1) * SCHm],
                                  deps=(ld_du, btoks[n]))
                        tsc = p.op("vector", "tensor_tensor_scan",
                                   v3(H, NST)[:, :, n], a_sb[:, asl_],
                                   b_sb[:, asl_],
                                   state[:, ct * NST + n:ct * NST + n + 1],
                                   ALU.mult, ALU.add, deps=(ta, st_tok[ct]))
                        tst = p.op("vector", "tensor_copy",
                                   state[:, ct * NST + n:ct * NST + n + 1],
                                   H[:, (SCHm - 1) * NST + n:
                                     (SCHm - 1) * NST + n + 1], deps=(tsc,))
                        tsts.append(tst)
                    st_tok[ct] = tsts[-1]
                    last_tail = tsts[-1]
                    m1_ = p.op("vector", "tensor_mul", v3(y1t, 8),
                               v3(H, NST)[:, :, 0:8], v3(crep, NST)[:, :, 0:8],
                               deps=(ctok,))
                    m2tok = p.op("vector", "tensor_mul", v3(y1b, 8),
                                 v3(H, NST)[:, :, 8:16],
                                 v3(crep, NST)[:, :, 8:16])
                    p.op("vector", "tensor_add", v3(y2t, 4),
                         v3(y1t, 8)[:, :, 0:4], v3(y1t, 8)[:, :, 4:8])
                    p.op("vector", "tensor_add", v3(y2b, 4),
                         v3(y1b, 8)[:, :, 0:4], v3(y1b, 8)[:, :, 4:8])
                    p.op("vector", "tensor_add", v3(y3v, 4),
                         v3(y2t, 4), v3(y2b, 4))
                    p.op("vector", "tensor_add", v3(y5v, 4)[:, 0:SCHm, 0:2],
                         v3(y3v, 4)[:, :, 0:2], v3(y3v, 4)[:, :, 2:4])
                    p.op("vector", "tensor_add", v3(y4v, 2)[:, 0:SCHm, 0],
                         v3(y5v, 4)[:, 0:SCHm, 0], v3(y5v, 4)[:, 0:SCHm, 1])
                    a5 = p.op("vector", "scalar_tensor_tensor",
                              v3(yyv, 2)[:, 0:SCHm, 0], xin_sb[:, csl],
                              dp_sb[:, ct:ct + 1], v3(y4v, 2)[:, 0:SCHm, 0],
                              ALU.mult, ALU.add, deps=(ld_xin, t_al))
                    a6 = p.op("vector", "tensor_mul", ygout[:, csl],
                              v3(yyv, 2)[:, 0:SCHm, 0], g_sb[:, csl],
                              deps=(ld_g, yg_sp[pp]))
                    ld_use[pp] = a6
                    yg_sp[pp] = p.dma("gpsimd", yg_con[cs, tsl], ygout[:, csl],
                                      deps=(a6,))
                brep_use = last_tail
                crep_use = m2tok
            A.reset(m1)
            agY = collective("AllGather", ALU.bypass, [yg_con], [yg_ag])
            drain_casts(20)

            m2 = A.mark()
            woutb = A.bf(2 * 2048)
            ygr = A.bf(4 * 8192)
            un_f = A.f32(2 * 512)
            un_b = A.bf(2 * 512)
            up_f = A.f32(2 * 512)
            last_layer = (li == NL - 1)
            evF = [agY, agY]
            un_sp = [None, None]
            up_use = [None, None]
            wld_prev = [None, None]
            yg_seg = yg_ag.rearrange("c (s w) -> (c s) w", w=512)
            ygld_tok = [[None] * 16 for _ in range(4)]
            for tt in range(4):
                for k in range(16):
                    base = k * 2048 + tt
                    ygld_tok[tt][k] = p.add(
                        "gpsimd", lambda E, base=base, tt=tt, k=k:
                        E.indirect_dma_start(
                            ygr[:, tt * 8192 + k * 512:tt * 8192 + (k + 1) * 512],
                            None, yg_seg,
                            IndirectOffsetOnAxis(ixq_sb[:], 0),
                            element_offset=base * 512),
                        deps=[agY, t_ix], dma=True)
            udst = u8_dr if last_layer else u_con
            for o in range(8):
                wp = o % 2
                wl = None
                for kk in range(2):
                    ld = p.dma("sync", stg[:, 0:1024],
                               ew[f"woutT{li}"].rearrange("(k p) d -> p k d", p=128)
                               [:, kk * 8:(kk + 1) * 8, o * 128:(o + 1) * 128],
                               deps=(wst_state["war"], wld_prev[wp], agY))
                    wl = p.op("vector", "tensor_copy",
                              woutb[:, wp * 2048 + kk * 1024:
                                    wp * 2048 + (kk + 1) * 1024],
                              stg[:, 0:1024], deps=(ld,))
                    wst_state["war"] = wl
                for tt in range(4):
                    pp = tt % 2
                    mmF = None
                    for k in range(16):
                        mmF = p.mm(psB[pp][:, 0:512],
                                   woutb[:, wp * 2048 + k * 128:
                                         wp * 2048 + (k + 1) * 128],
                                   ygr[:, tt * 8192 + k * 512:
                                       tt * 8192 + (k + 1) * 512],
                                   start=(k == 0), stop=(k == 15),
                                   deps=(wl, ygld_tok[tt][k], evF[pp]),
                                   inc=(k == 15))
                    wld_prev[wp] = mmF
                    ldu = p.dma("sync", up_f[:, pp * 512:(pp + 1) * 512],
                                u_con[o * 128:(o + 1) * 128,
                                      tt * 512:(tt + 1) * 512],
                                deps=(up_use[pp], agY))
                    ev = p.op("vector", "tensor_add",
                              un_f[:, pp * 512:(pp + 1) * 512],
                              up_f[:, pp * 512:(pp + 1) * 512], psB[pp][:, 0:512],
                              deps=(mmF, ldu, un_sp[pp]))
                    evF[pp] = ev
                    up_use[pp] = ev
                    s1_ = p.dma("gpsimd",
                                udst[o * 128:(o + 1) * 128,
                                     tt * 512:(tt + 1) * 512],
                                un_f[:, pp * 512:(pp + 1) * 512], deps=(ev, ldu))
                    un_sp[pp] = s1_
                    if not last_layer:
                        cb_ = p.op("vector", "tensor_copy",
                                   un_b[:, pp * 512:(pp + 1) * 512],
                                   un_f[:, pp * 512:(pp + 1) * 512])
                        un_sp[pp] = p.dma("gpsimd",
                                          ub_con[o * 128:(o + 1) * 128,
                                                 tt * 512:(tt + 1) * 512],
                                          un_b[:, pp * 512:(pp + 1) * 512],
                                          deps=(cb_,))
            A.reset(m2)
            if not last_layer:
                agU = collective("AllGather", ALU.bypass, [u_con], [u_ag])
                agU = collective("AllGather", ALU.bypass, [ub_con], [ub_ag])

        def dump_stop(dbgs):
            bb = barrier()
            tk = None
            for nm, t in dbgs:
                o_ = nc.dram_tensor(nm, list(t.shape), t.dtype,
                                    kind="ExternalOutput").ap()
                tk = p.dma("gpsimd", o_, t, deps=bb)
            p.op("gpsimd", "engine_nop", deps=(tk,))

        if STOP == "mamba":
            dump_stop([("d_u8", u8_dr), ("d_dl", dl_dr), ("d_xin", xin_dr),
                       ("d_g", g_dr), ("d_b", b_dr), ("d_yg", yg_ag),
                       ("d_xd", xd_ar), ("d_cp", cpos_dr), ("d_uag", ub_ag)])
            return
        t_x1 = fm_norm(x1_dr, u8_dr, res_dr=x0q, dep0=barrier())
        drain_casts(10 ** 9)

        if STOP == "p1":
            dump_stop([("d_x1", x1_dr)])
            return
        t_v2 = fm_norm(v2f_dr, x1_dr, bf_dr=v2b_dr, dep0=(t_x1,))
        sc_t = None
        for o in range(32):
            p.dma("sync", wcl[:, o:o + 1], ew["fs1"][o * 128:(o + 1) * 128, :],
                  deps=(nwar["st"][0], nwar["st"][1]))
            p.dma("sync", wcl[:, 32 + o:33 + o],
                  ew["fs2"][o * 128:(o + 1) * 128, :])
            p.dma("sync", wcl[:, 64 + o:65 + o],
                  ew["fsh1"][o * 128:(o + 1) * 128, :])
            sc_t = p.dma("sync", wcl[:, 96 + o:97 + o],
                         ew["fsh2"][o * 128:(o + 1) * 128, :])
        for o in range(8):
            p.dma("sync", wcl[:, 128 + o:129 + o],
                  ew["fs3"][o * 128:(o + 1) * 128, :])
            p.dma("sync", wcl[:, 136 + o:137 + o],
                  ew["ln1w"][0:1, o * 128:(o + 1) * 128]
                  .transpose([1, 0]))
            sc_t = p.dma("sync", wcl[:, 144 + o:145 + o],
                         ew["ln2w"][0:1, o * 128:(o + 1) * 128]
                         .transpose([1, 0]))

        m3 = A.mark()
        hA = A.bf(32 * 512)
        hB = A.bf(32 * 512)
        hC = A.bf(32 * 512)
        wstb = A.bf(2 * 4096)
        vt_sb = A.bf(8 * 512)
        fo_f = A.f32(2 * 512)
        evP = [None, None]
        wld_prevP = [None, None]

        def dense(out_cb, out_blocks, in_view, in_blocks, wsrc, w_is_bf, scol,
                  func, in_tok, out_war=None):
            toks = []
            for o in range(out_blocks):
                pp = o % 2
                if w_is_bf:
                    wld = p.dma("sync",
                                wstb[:, pp * 4096:pp * 4096 + in_blocks * 128]
                                .rearrange("p (k c) -> p k c", k=in_blocks),
                                wsrc.rearrange("(k p) e -> p k e", p=128)
                                [:, :, o * 128:(o + 1) * 128],
                                deps=(wld_prevP[pp], cst["st"]))
                else:
                    wld = None
                    for kk in range(in_blocks // 8):
                        ld = p.dma("sync", stg[:, 0:1024],
                                   wsrc.rearrange("(k p) e -> p k e", p=128)
                                   [:, kk * 8:(kk + 1) * 8, o * 128:(o + 1) * 128],
                                   deps=(wst_state["war"], wld_prevP[pp]))
                        wld = p.op("vector", "tensor_copy",
                                   wstb[:, pp * 4096 + kk * 1024:
                                        pp * 4096 + (kk + 1) * 1024],
                                   stg[:, 0:1024], deps=(ld,))
                        wst_state["war"] = wld
                mm = None
                for k in range(in_blocks):
                    mm = p.mm(psA[pp][:, 0:512],
                              wstb[:, pp * 4096 + k * 128:pp * 4096 + (k + 1) * 128],
                              in_view(k),
                              start=(k == 0), stop=(k == in_blocks - 1),
                              deps=(wld, evP[pp], in_tok), inc=(k == in_blocks - 1))
                wld_prevP[pp] = mm
                dst, kind = out_cb(o)
                if kind == "act":
                    ev = p.op("scalar", "activation", dst, psA[pp][:, 0:512],
                              func, scale=wcl[:, scol + o:scol + o + 1],
                              deps=(mm, out_war, sc_t))
                elif kind == "vsc":
                    ev = p.op("vector", "tensor_scalar", dst, psA[pp][:, 0:512],
                              wcl[:, scol + o:scol + o + 1], None, op0=ALU.mult,
                              deps=(mm, out_war, sc_t))
                else:
                    ev = p.op("vector", "tensor_copy", dst, psA[pp][:, 0:512],
                              deps=(mm, out_war))
                toks.append(ev)
                evP[pp] = ev
            return toks

        hC_read = None
        vld_war = None
        fo_sp = [None, None]
        for tt in range(4):
            tcols = slice(tt * 512, (tt + 1) * 512)
            vld = None
            for k in range(8):
                vld = p.dma("sync", vt_sb[:, k * 512:(k + 1) * 512],
                            v2b_dr[k * 128:(k + 1) * 128, tcols],
                            deps=(vld_war, t_v2))
            t_h1 = dense(lambda o: (hA[:, o * 512:(o + 1) * 512], "act"),
                         32, lambda k: vt_sb[:, k * 512:(k + 1) * 512], 8,
                         ew["fw1T"], False, 0, AF.Silu, vld)
            t_g2 = dense(lambda o: (hB[:, o * 512:(o + 1) * 512], "vsc"),
                         32, lambda k: vt_sb[:, k * 512:(k + 1) * 512], 8,
                         ew["fw2T"], False, 32, AF.Copy, vld)
            vld_war = wld_prevP[0]
            t_h = None
            for o in range(32):
                t_h = p.op("vector", "tensor_mul", hC[:, o * 512:(o + 1) * 512],
                           hA[:, o * 512:(o + 1) * 512],
                           hB[:, o * 512:(o + 1) * 512],
                           deps=(t_h1[o], t_g2[o], hC_read))
            t_h2 = dense(lambda o: (hA[:, o * 512:(o + 1) * 512], "act"),
                         32, lambda k: hC[:, k * 512:(k + 1) * 512], 32,
                         h1c_dr, True, 64, AF.Gelu, t_h, out_war=t_h)
            hC_read = wld_prevP[(31) % 2]
            t_h3 = dense(lambda o: (hB[:, o * 512:(o + 1) * 512], "act"),
                         32, lambda k: hA[:, k * 512:(k + 1) * 512], 32,
                         h2c_dr, True, 96, AF.Gelu, t_h2[-1], out_war=t_h)
            def fo_cb(o):
                return (fo_f[:, (o % 2) * 512:(o % 2) * 512 + 512], "vsc")
            t_f = dense(fo_cb, 8, lambda k: hB[:, k * 512:(k + 1) * 512], 32,
                        ew["fw3T"], False, 128, AF.Copy, t_h3[-1],
                        out_war=None)
            for o in range(8):
                fo_sp[o % 2] = p.dma("gpsimd",
                                     f_dr[o * 128:(o + 1) * 128, tcols],
                                     fo_f[:, (o % 2) * 512:(o % 2) * 512 + 512],
                                     deps=(t_f[o],))
        A.reset(m3)
        t_x2 = fm_norm(x2_dr, f_dr, res_dr=x1_dr, dep0=(fo_sp[0], fo_sp[1]))

        if STOP == "p2":
            dump_stop([("d_x2", x2_dr), ("d_f", f_dr)])
            return
        t_v3 = fm_norm(v2f_dr, x2_dr, bf_dr=v2b_dr, dep0=(t_x2,))
        m4 = A.mark()
        h_sb = A.bf(8 * TLOC)
        wstb3 = A.bf(2 * 1024)
        vt3 = A.bf(8 * 512)
        b3 = A.f32(8)
        go_f = A.f32(2 * 512)
        btok = None
        for o in range(8):
            btok = p.dma("sync", b3[:, o:o + 1], ew["l1b"][o * 128:(o + 1) * 128, :],
                         deps=(nwar["st"][0], nwar["st"][1]))
        evP3 = [None, None]
        wld3 = [None, None]
        vld_war = None
        h_done = []
        for tt in range(4):
            tcols = slice(tt * 512, (tt + 1) * 512)
            vld = None
            for k in range(8):
                vld = p.dma("sync", vt3[:, k * 512:(k + 1) * 512],
                            v2b_dr[k * 128:(k + 1) * 128, tcols],
                            deps=(vld_war, t_v3))
            for o in range(8):
                pp = o % 2
                wld = None
                for kk in range(1):
                    ld = p.dma("sync", stg[:, 0:1024],
                               ew["l1wT"].rearrange("(k p) e -> p k e", p=128)
                               [:, 0:8, o * 128:(o + 1) * 128],
                               deps=(wst_state["war"], wld3[pp]))
                    wld = p.op("vector", "tensor_copy",
                               wstb3[:, pp * 1024:(pp + 1) * 1024],
                               stg[:, 0:1024], deps=(ld,))
                    wst_state["war"] = wld
                mm = None
                for k in range(8):
                    mm = p.mm(psA[pp][:, 0:512],
                              wstb3[:, pp * 1024 + k * 128:pp * 1024 + (k + 1) * 128],
                              vt3[:, k * 512:(k + 1) * 512],
                              start=(k == 0), stop=(k == 7),
                              deps=(wld, vld, evP3[pp]), inc=(k == 7))
                wld3[pp] = mm
                ev = p.op("vector", "tensor_scalar",
                          h_sb[:, o * TLOC + tt * 512:o * TLOC + tt * 512 + 512],
                          psA[pp][:, 0:512], b3[:, o:o + 1], None, op0=ALU.add,
                          deps=(mm, btok))
                evP3[pp] = ev
                h_done.append(ev)
            vld_war = mm
        hd = (h_done[-1],)
        go_sp = [None, None]
        evC3 = [None, None]
        for mo in range(8):
            for tt in range(4):
                pp = (mo * 4 + tt) % 2
                mm = None
                for ko in range(mo + 1):
                    lhs = triu_b[:] if ko == mo else ones_b[:]
                    mm = p.mm(psB[pp][:, 0:512], lhs,
                              h_sb[:, ko * TLOC + tt * 512:ko * TLOC + tt * 512 + 512],
                              start=(ko == 0), stop=(ko == mo),
                              deps=hd + (t_onesb, t_tri, evC3[pp]), inc=(ko == mo))
                ev = p.op("vector", "tensor_copy", go_f[:, pp * 512:pp * 512 + 512],
                          psB[pp][:, 0:512], deps=(mm, go_sp[pp]))
                evC3[pp] = ev
                go_sp[pp] = p.dma("gpsimd",
                                  f_dr[mo * 128:(mo + 1) * 128, tt * 512:tt * 512 + 512],
                                  go_f[:, pp * 512:pp * 512 + 512], deps=(ev,))
        A.reset(m4)
        t_x3 = fm_norm(x3_dr, f_dr, res_dr=x2_dr, dep0=(go_sp[0], go_sp[1]))

        if STOP == "p3":
            dump_stop([("d_x3", x3_dr)])
            return
        t_y1 = fm_norm(y1_dr, x3_dr, dep0=(t_x3,))
        t_r1 = fm_norm(v2f_dr, y1_dr, wcol=wcl[:, 136:144], kind="rms",
                       bf_dr=v2b_dr, dep0=(t_y1,))
        m5 = A.mark()
        wstb4 = A.bf(2 * 1024)
        vt4 = A.bf(8 * 2048)
        o4 = A.bf(4 * 512)
        vo4 = A.bf(2 * 1024)
        vld = None
        for o in range(8):
            for tt in range(4):
                vld = p.dma("sync",
                            vt4[:, o * TLOC + tt * 512:o * TLOC + tt * 512 + 512],
                            v2b_dr[o * 128:(o + 1) * 128, tt * 512:tt * 512 + 512],
                            deps=(t_r1,))
        evQ = [None, None]
        wld4 = [None, None]
        out_dma4 = [None, None, None, None]
        for wi, (wnm, dst) in enumerate((("wqT", q_dr), ("wkT", k_con))):
            for o in range(8):
                wp = o % 2
                ld = p.dma("sync", stg[:, 0:1024],
                           ew[wnm].rearrange("(k p) e -> p k e", p=128)
                           [:, 0:8, o * 128:(o + 1) * 128],
                           deps=(wst_state["war"], wld4[wp]))
                wld = p.op("vector", "tensor_copy",
                           wstb4[:, wp * 1024:(wp + 1) * 1024], stg[:, 0:1024],
                           deps=(ld,))
                wst_state["war"] = wld
                for tt in range(4):
                    pp = tt % 2
                    sp = wi * 2 + pp
                    mm = None
                    for k in range(8):
                        mm = p.mm(psA[pp][:, 0:512],
                                  wstb4[:, wp * 1024 + k * 128:wp * 1024 + (k + 1) * 128],
                                  vt4[:, k * TLOC + tt * 512:k * TLOC + tt * 512 + 512],
                                  start=(k == 0), stop=(k == 7),
                                  deps=(wld, vld, evQ[pp]), inc=(k == 7))
                    if tt == 3:
                        wld4[wp] = mm
                    ev = p.op("vector", "tensor_copy", o4[:, sp * 512:sp * 512 + 512],
                              psA[pp][:, 0:512], deps=(mm, out_dma4[sp]))
                    evQ[pp] = ev
                    out_dma4[sp] = p.dma("gpsimd",
                                         dst[o * 128:(o + 1) * 128,
                                             tt * 512:tt * 512 + 512],
                                         o4[:, sp * 512:sp * 512 + 512], deps=(ev,))
        wvb = A.bf(8 * 1024)
        wvtok = None
        for k in range(8):
            ld = p.dma("sync", stg[:, 0:1024],
                       ew["wvT"][k * 128:(k + 1) * 128, :],
                       deps=(wst_state["war"],))
            wvtok = p.op("vector", "tensor_copy", wvb[:, k * 1024:(k + 1) * 1024],
                         stg[:, 0:1024], deps=(ld,))
            wst_state["war"] = wvtok
        vev = [None, None]
        vdma = [None, None]
        for tc in range(16):
            pp = tc % 2
            mm = None
            for half in range(2):
                for k in range(8):
                    mm = p.mm(psB[pp][:, 0:512],
                              vt4[:, k * TLOC + tc * 128:k * TLOC + (tc + 1) * 128],
                              wvb[:, k * 1024 + half * 512:k * 1024 + half * 512 + 512],
                              start=(k == 0), stop=(k == 7),
                              deps=(wvtok, vld, vev[pp]), inc=(k == 7))
                ev = p.op("vector", "tensor_copy",
                          vo4[:, pp * 1024 + half * 512:pp * 1024 + half * 512 + 512],
                          psB[pp][:, 0:512], deps=(mm, vdma[pp]))
                vev[pp] = ev
            vdma[pp] = p.dma("gpsimd", v_con[tc * 128:(tc + 1) * 128, :],
                             vo4[:, pp * 1024:pp * 1024 + 1024], deps=(vev[pp],))
        A.reset(m5)
        agK = collective("AllGather", ALU.bypass, [k_con], [k_ag])
        agV = collective("AllGather", ALU.bypass, [v_con], [v_ag])

        m6 = A.mark()
        q_sb = A.bf(2 * TLOC)
        kh_sb = A.bf(2 * S)
        vh_sb = A.bf(2 * 65 * 64)
        pt_sb = A.bf(2 * 512)
        ao_sb = A.bf(8 * TLOC)
        den_st = A.f32(2 * TLOC)
        den_rc = A.f32(2 * TLOC)
        rdrep = A.f32(TLOC)
        y1l = A.f32(2 * 512)
        ho_f = A.f32(2 * 512)
        wstb5 = A.bf(2 * 1024)
        NKC_ = S // 128
        q_use = [None, None]
        kh_use = [None, None]
        vh_use = [None, None]
        rd_dma = [None, None]
        pt_use = [None, None]
        qk_evac = [None, None]
        pv_evac = [None, None]
        mmqk = mmpv = None
        for h in range(16):
            hp = h % 2
            qtok = p.dma("sync", q_sb[0:64, hp * TLOC:(hp + 1) * TLOC],
                         q_dr[h * 64:(h + 1) * 64, :],
                         deps=(q_use[hp], out_dma4[0], out_dma4[1]))
            kld = None
            for r in range(4):
                kld = p.dma("sync",
                            kh_sb[:, hp * S + r * TLOC:hp * S + (r + 1) * TLOC]
                            [0:64, :],
                            k_ag[r * D + h * 64:r * D + (h + 1) * 64, :],
                            deps=(kh_use[hp], agK))
            vms = p.op("gpsimd", "memset",
                       vh_sb[:, hp * 65 * 64:(hp + 1) * 65 * 64], 1.0,
                       deps=(vh_use[hp],))
            vld2 = None
            for ch in range(NKC_):
                vld2 = p.dma("sync",
                             vh_sb[:, hp * 65 * 64 + ch * 65:
                                   hp * 65 * 64 + ch * 65 + 64],
                             v_ag[ch * 128:(ch + 1) * 128, h * 64:(h + 1) * 64],
                             deps=(vms, agV))
            for tqi in range(4):
                pp = tqi % 2
                for ch in range(NKC_):
                    cp = ch % 2
                    mmqk = p.mm(psA[cp][:, 0:512],
                                kh_sb[0:64, hp * S + ch * 128:hp * S + (ch + 1) * 128],
                                q_sb[0:64, hp * TLOC + tqi * 512:
                                     hp * TLOC + tqi * 512 + 512],
                                start=True, stop=True,
                                deps=(kld, qtok, qk_evac[cp]), inc=True)
                    exv = p.op("scalar", "activation", pt_sb[:, cp * 512:cp * 512 + 512],
                               psA[cp][:, 0:512], AF.Exp, scale=0.125,
                               bias=mb40[:, 0:1],
                               deps=(mmqk, t_m40, pt_use[cp]))
                    qk_evac[cp] = exv
                    mmpv = p.mm(ps65[pp][:, 0:512],
                                vh_sb[:, hp * 65 * 64 + ch * 65:
                                      hp * 65 * 64 + (ch + 1) * 65],
                                pt_sb[:, cp * 512:cp * 512 + 512],
                                start=(ch == 0), stop=(ch == NKC_ - 1),
                                deps=(vld2, exv, pv_evac[pp]), inc=True)
                    pt_use[cp] = mmpv
                ev = p.op("vector", "tensor_copy",
                          ao_sb[:, (h // 2) * TLOC + tqi * 512:
                                (h // 2) * TLOC + tqi * 512 + 512]
                          [(h % 2) * 64:(h % 2) * 64 + 64, :],
                          ps65[pp][0:64, 0:512], deps=(mmpv,))
                den_cp = p.op("vector", "tensor_copy",
                              den_st[0:1, hp * TLOC + tqi * 512:
                                     hp * TLOC + tqi * 512 + 512],
                              ps65[pp][64:65, 0:512], deps=(rd_dma[hp],))
                pv_evac[pp] = den_cp
            kh_use[hp] = mmqk
            q_use[hp] = mmqk
            vh_use[hp] = mmpv
            rcp = p.op("vector", "reciprocal",
                       den_rc[0:1, hp * TLOC:(hp + 1) * TLOC],
                       den_st[0:1, hp * TLOC:(hp + 1) * TLOC],
                       deps=(pv_evac[0], pv_evac[1], rd_dma[hp]))
            rd_dma[hp] = p.dma("gpsimd", rd_dr[h:h + 1, :],
                               den_rc[0:1, hp * TLOC:(hp + 1) * TLOC],
                               deps=(rcp,))
        rdma = rd_dma[0] if rd_dma[0][1] > rd_dma[1][1] else rd_dma[1]
        sctok = None
        rdrep_use = None
        for rt in range(8):
            bc0 = p.dma("sync", rdrep[0:64, :],
                        rd_dr[2 * rt:2 * rt + 1, :].partition_broadcast(64)
                        .squeeze(1), deps=(rdma, rdrep_use))
            bc1 = p.dma("sync", rdrep[64:128, :],
                        rd_dr[2 * rt + 1:2 * rt + 2, :].partition_broadcast(64)
                        .squeeze(1), deps=(rdma, rdrep_use))
            sc = p.op("vector", "tensor_mul", ao_sb[:, rt * TLOC:(rt + 1) * TLOC],
                      ao_sb[:, rt * TLOC:(rt + 1) * TLOC], rdrep[:],
                      deps=(bc0, bc1))
            rdrep_use = sc
        sctok = rdrep_use
        evW = [None, None]
        ho_dma = [None, None]
        y1_use = [None, None]
        wldW = [None, None]
        for o in range(8):
            wp = o % 2
            ld = p.dma("sync", stg[:, 0:1024],
                       ew["woT"].rearrange("(k p) e -> p k e", p=128)
                       [:, 0:8, o * 128:(o + 1) * 128],
                       deps=(wst_state["war"], wldW[wp]))
            wld = p.op("vector", "tensor_copy",
                       wstb5[:, wp * 1024:(wp + 1) * 1024], stg[:, 0:1024],
                       deps=(ld,))
            wst_state["war"] = wld
            for tt in range(4):
                pp = tt % 2
                yld = p.dma("sync", y1l[:, pp * 512:pp * 512 + 512],
                            y1_dr[o * 128:(o + 1) * 128, tt * 512:tt * 512 + 512],
                            deps=(y1_use[pp],))
                mm = None
                for k in range(8):
                    mm = p.mm(psB[pp][:, 0:512],
                              wstb5[:, wp * 1024 + k * 128:wp * 1024 + (k + 1) * 128],
                              ao_sb[:, k * TLOC + tt * 512:k * TLOC + tt * 512 + 512],
                              start=(k == 0), stop=(k == 7),
                              deps=(wld, sctok, evW[pp]), inc=(k == 7))
                if tt == 3:
                    wldW[wp] = mm
                ev = p.op("vector", "tensor_add", ho_f[:, pp * 512:pp * 512 + 512],
                          y1l[:, pp * 512:pp * 512 + 512], psB[pp][:, 0:512],
                          deps=(mm, yld, ho_dma[pp]))
                evW[pp] = ev
                y1_use[pp] = ev
                ho_dma[pp] = p.dma("gpsimd",
                                   h4_dr[o * 128:(o + 1) * 128,
                                         tt * 512:tt * 512 + 512],
                                   ho_f[:, pp * 512:pp * 512 + 512], deps=(ev,))
        A.reset(m6)

        if STOP == "p4a":
            dump_stop([("d_h4", h4_dr)])
            return
        t_r2 = fm_norm(v2f_dr, h4_dr, wcol=wcl[:, 144:152], kind="rms",
                       bf_dr=v2b_dr, dep0=(ho_dma[0], ho_dma[1]))
        m7 = A.mark()
        r_bf = A.bf(8 * TLOC)
        gt_sb = A.bf(8 * 8)
        lg_sb = A.f32(128)
        m1s = A.f32(16)
        m2s = A.f32(16)
        s1s = A.f32(16)
        s2s = A.f32(16)
        lp_sb = A.f32(128)
        is0_sb = A.f32(128)
        tmp_sb = A.f32(128)
        is2_sb = A.f32(128)
        we_sb = A.f32(128)
        weT_sb = A.f32(16 * 128)
        werep = A.f32(TLOC)
        h1m = A.bf(16 * 512)
        g3m = A.bf(16 * 512)
        accm = A.f32(8 * TLOC)
        hr_l = A.f32(2 * 512)
        om_f = A.f32(2 * 512)
        wstb6 = A.bf(2 * 2048)
        rld = None
        for o in range(8):
            for tt in range(4):
                rld = p.dma("sync",
                            r_bf[:, o * TLOC + tt * 512:o * TLOC + tt * 512 + 512],
                            v2b_dr[o * 128:(o + 1) * 128, tt * 512:tt * 512 + 512],
                            deps=(t_r2,))
        gtok = None
        for k in range(8):
            ld = p.dma("sync", stg[:, 0:8], ew["gateT"][k * 128:(k + 1) * 128, :],
                       deps=(wst_state["war"],))
            gtok = p.op("vector", "tensor_copy", gt_sb[:, k * 8:(k + 1) * 8],
                        stg[:, 0:8], deps=(ld,))
            wst_state["war"] = gtok
        lgev = None
        for tc in range(16):
            mm = None
            for k in range(8):
                mm = p.mm(psA[0][:, 0:8],
                          r_bf[:, k * TLOC + tc * 128:k * TLOC + (tc + 1) * 128],
                          gt_sb[:, k * 8:(k + 1) * 8],
                          start=(k == 0), stop=(k == 7),
                          deps=(rld, gtok, lgev), inc=(k == 7))
            lgev = p.op("vector", "tensor_copy", lg_sb[:, tc * 8:(tc + 1) * 8],
                        psA[0][:, 0:8], deps=(mm,))

        def r3(t, n):
            return t.rearrange("p (a b) -> p a b", b=n)

        t0g = p.op("vector", "tensor_reduce", m1s[:], r3(lg_sb, 8), AX.X,
                   ALU.max, deps=(lgev,))
        p.op("vector", "tensor_tensor", r3(lp_sb, 8), r3(lg_sb, 8),
             m1s[:].unsqueeze(2).broadcast_to([128, 16, 8]), ALU.subtract)
        p.op("vector", "tensor_single_scalar", is0_sb[:], lp_sb[:], 0.0,
             ALU.is_equal)
        p.op("vector", "scalar_tensor_tensor", tmp_sb[:], is0_sb[:], -1e30,
             lp_sb[:], ALU.mult, ALU.add)
        t4g = p.op("vector", "tensor_reduce", m2s[:], r3(tmp_sb, 8), AX.X,
                   ALU.max)
        t5g = p.op("scalar", "activation", s2s[:], m2s[:], AF.Sigmoid,
                   deps=(t4g,))
        p.op("vector", "tensor_scalar", s1s[:], s2s[:], -1.0, 1.0,
             op0=ALU.mult, op1=ALU.add, deps=(t5g,))
        p.op("vector", "tensor_tensor", r3(is2_sb, 8), r3(tmp_sb, 8),
             m2s[:].unsqueeze(2).broadcast_to([128, 16, 8]), ALU.is_equal)
        p.op("vector", "tensor_tensor", r3(we_sb, 8), r3(is0_sb, 8),
             s1s[:].unsqueeze(2).broadcast_to([128, 16, 8]), ALU.mult)
        p.op("vector", "tensor_tensor", r3(is2_sb, 8), r3(is2_sb, 8),
             s2s[:].unsqueeze(2).broadcast_to([128, 16, 8]), ALU.mult)
        t10g = p.op("vector", "tensor_add", we_sb[:], we_sb[:], is2_sb[:])
        wetok = None
        for tc in range(16):
            mmt = p.add("tensor", lambda E, tc=tc: E.transpose(
                psT4[:], we_sb[:, tc * 8:(tc + 1) * 8], ident[:]),
                deps=[t10g, t_id, wetok], serial=False)
            wetok = p.op("vector", "tensor_copy",
                         weT_sb[0:8, tc * 128:(tc + 1) * 128],
                         psT4[:], deps=(mmt,))
        wedma = p.dma("gpsimd", we_dr, weT_sb[0:8, :], deps=(wetok,))

        evM = [None, None]
        wldM = None
        h2_dma = [None, None]
        hr_use = [None, None]
        werep_use = None
        acc_tok = {}
        h1_read = None
        for e in range(8):
            bcw = p.dma("sync", werep[:, :],
                        we_dr[e:e + 1, :].partition_broadcast(128).squeeze(1),
                        deps=(wedma, werep_use))
            for tt in range(4):
                tcols = slice(tt * 512, tt * 512 + 512)
                toks_h = []
                for o in range(16):
                    pp = o % 2
                    wld = p.dma("sync", wstb6[:, pp * 2048:pp * 2048 + 1024]
                                .rearrange("p (k c) -> p k c", k=8),
                                m1c_dr.rearrange("(k p) e2 -> p k e2", p=128)
                                [:, e * 8:(e + 1) * 8, o * 128:(o + 1) * 128],
                                deps=(wldM, cst["st"]))
                    mm = None
                    for k in range(8):
                        mm = p.mm(psA[pp][:, 0:512],
                                  wstb6[:, pp * 2048 + k * 128:pp * 2048 + (k + 1) * 128],
                                  r_bf[:, k * TLOC + tt * 512:k * TLOC + tt * 512 + 512],
                                  start=(k == 0), stop=(k == 7),
                                  deps=(wld, rld, evM[pp]), inc=(k == 7))
                    ev = p.op("scalar", "activation", h1m[:, o * 512:o * 512 + 512],
                              psA[pp][:, 0:512], AF.Silu, deps=(mm, h1_read))
                    evM[pp] = ev
                    toks_h.append(ev)
                    wld2 = p.dma("sync", wstb6[:, pp * 2048 + 1024:pp * 2048 + 2048]
                                 .rearrange("p (k c) -> p k c", k=8),
                                 m3c_dr.rearrange("(k p) e2 -> p k e2", p=128)
                                 [:, e * 8:(e + 1) * 8, o * 128:(o + 1) * 128],
                                 deps=(wldM, cst["st"]))
                    mm2 = None
                    for k in range(8):
                        mm2 = p.mm(psA[pp][:, 0:512],
                                   wstb6[:, pp * 2048 + 1024 + k * 128:
                                         pp * 2048 + 1024 + (k + 1) * 128],
                                   r_bf[:, k * TLOC + tt * 512:k * TLOC + tt * 512 + 512],
                                   start=(k == 0), stop=(k == 7),
                                   deps=(wld2, rld, ev), inc=(k == 7))
                    wldM = mm2
                    ev2 = p.op("vector", "tensor_copy", g3m[:, o * 512:o * 512 + 512],
                               psA[pp][:, 0:512], deps=(mm2, h1_read))
                    ev3 = p.op("vector", "tensor_mul", h1m[:, o * 512:o * 512 + 512],
                               h1m[:, o * 512:o * 512 + 512],
                               g3m[:, o * 512:o * 512 + 512], deps=(ev, ev2))
                    ev4 = p.op("vector", "tensor_mul", h1m[:, o * 512:o * 512 + 512],
                               h1m[:, o * 512:o * 512 + 512], werep[:, tcols],
                               deps=(bcw,))
                    toks_h.append(ev4)
                he_tok = toks_h[-1]
                for o2 in range(8):
                    pp = o2 % 2
                    wld3 = p.dma("sync", wstb6[:, pp * 2048:pp * 2048 + 2048]
                                 .rearrange("p (k c) -> p k c", k=16),
                                 m2c_dr.rearrange("(k p) d -> p k d", p=128)
                                 [:, e * 16:(e + 1) * 16, o2 * 128:(o2 + 1) * 128],
                                 deps=(wldM, cst["st"]))
                    mm3 = None
                    for k in range(16):
                        mm3 = p.mm(psB[pp][:, 0:512],
                                   wstb6[:, pp * 2048 + k * 128:pp * 2048 + (k + 1) * 128],
                                   h1m[:, k * 512:k * 512 + 512],
                                   start=(k == 0), stop=(k == 15),
                                   deps=(wld3, he_tok, evM[pp]), inc=(k == 15))
                    wldM = mm3
                    aslot = accm[:, o2 * TLOC + tt * 512:o2 * TLOC + tt * 512 + 512]
                    if e == 0:
                        ev5 = p.op("vector", "tensor_copy", aslot, psB[pp][:, 0:512],
                                   deps=(mm3,))
                    else:
                        ev5 = p.op("vector", "tensor_add", aslot, aslot,
                                   psB[pp][:, 0:512],
                                   deps=(mm3, acc_tok.get((o2, tt))))
                    evM[pp] = ev5
                    acc_tok[(o2, tt)] = ev5
                h1_read = mm3
            werep_use = he_tok
        for o2 in range(8):
            for tt in range(4):
                pp = tt % 2
                yld = p.dma("sync", hr_l[:, pp * 512:pp * 512 + 512],
                            h4_dr[o2 * 128:(o2 + 1) * 128, tt * 512:tt * 512 + 512],
                            deps=(hr_use[pp],))
                ev = p.op("vector", "tensor_add", om_f[:, pp * 512:pp * 512 + 512],
                          hr_l[:, pp * 512:pp * 512 + 512],
                          accm[:, o2 * TLOC + tt * 512:o2 * TLOC + tt * 512 + 512],
                          deps=(yld, acc_tok[(o2, tt)], h2_dma[pp]))
                hr_use[pp] = ev
                h2_dma[pp] = p.dma("gpsimd",
                                   f_dr[o2 * 128:(o2 + 1) * 128,
                                        tt * 512:tt * 512 + 512],
                                   om_f[:, pp * 512:pp * 512 + 512], deps=(ev,))
        A.reset(m7)
        if STOP == "p4m":
            dump_stop([("d_h2", f_dr)])
            return
        fin = fm_norm(out_o, f_dr, res_dr=x3_dr, dep0=(h2_dma[0], h2_dma[1]))
        p.op("gpsimd", "engine_nop", deps=barrier())

    return build_program(fill)


def _mega_maps(inp):
    x0 = inp["L0_attn_x"].astype(np.float32, copy=False)
    x0_fm = np.ascontiguousarray(x0.reshape(B * S, D).T)
    AC = np.ascontiguousarray
    shared = {}
    shared["fw1T"] = AC(inp["L0_ffn_w1_w"].T)
    shared["fw2T"] = AC(inp["L0_ffn_w2_w"].T)
    shared["fh1T"] = AC(inp["L0_ffn_h1_w"].T)
    shared["fh2T"] = AC(inp["L0_ffn_h2_w"].T)
    shared["fw3T"] = AC(inp["L0_ffn_w3_w"].T)
    shared["fs1"] = AC(inp["L0_ffn_w1_s"].reshape(-1, 1))
    shared["fs2"] = AC(inp["L0_ffn_w2_s"].reshape(-1, 1))
    shared["fsh1"] = AC(inp["L0_ffn_h1_s"].reshape(-1, 1))
    shared["fsh2"] = AC(inp["L0_ffn_h2_s"].reshape(-1, 1))
    shared["fs3"] = AC(inp["L0_ffn_w3_s"].reshape(-1, 1))
    shared["l1wT"] = AC(inp["L1_attn_W"].T)
    shared["l1b"] = AC(inp["L1_attn_bias"].reshape(-1, 1))
    shared["ln1w"] = AC(inp["L1_ffn_ln1_w"].reshape(1, -1))
    shared["ln2w"] = AC(inp["L1_ffn_ln2_w"].reshape(1, -1))
    shared["wqT"] = AC(inp["L1_ffn_wq"].T)
    shared["wkT"] = AC(inp["L1_ffn_wk"].T)
    shared["wvT"] = AC(inp["L1_ffn_wv"].T)
    shared["woT"] = AC(inp["L1_ffn_wo"].T)
    shared["gateT"] = AC(inp["L1_ffn_gate_w"].T)
    shared["mw1T"] = AC(np.concatenate([inp["L1_ffn_w1"][e].T for e in range(8)], 0))
    shared["mw3T"] = AC(np.concatenate([inp["L1_ffn_w3"][e].T for e in range(8)], 0))
    shared["mw2T"] = AC(np.concatenate([inp["L1_ffn_w2"][e].T for e in range(8)], 0))
    maps = []
    for c in range(NCORES):
        q = c % 4
        cs = slice(q * CSH, (q + 1) * CSH)
        m = dict(shared)
        m["x0q"] = AC(x0_fm[:, c * TLOC:(c + 1) * TLOC])
        m["ixq"] = (np.arange(128, dtype=np.int32) * 16 + q * 4).reshape(128, 1)
        for i in range(NLAYER):
            WiT = inp["L0_attn_W_in"][i].T
            m[f"winT{i}"] = AC(np.concatenate(
                [WiT[:, q * CSH:(q + 1) * CSH],
                 WiT[:, DIN + q * CSH:DIN + (q + 1) * CSH]], 1))
            m[f"woutT{i}"] = AC(inp["L0_attn_W_out"][i].T)
            m[f"wxT{i}"] = AC(inp["L0_attn_W_x"][i].T[cs])
            m[f"wdtT{i}"] = AC(inp["L0_attn_W_dt"][i].T[:, cs])
            m[f"convw{i}"] = AC(inp["L0_attn_conv_w"][i][cs])
            m[f"convb{i}"] = AC(inp["L0_attn_conv_b"][i][cs].reshape(-1, 1))
            m[f"bdt{i}"] = AC(inp["L0_attn_b_dt"][i][cs].reshape(-1, 1))
            m[f"alog{i}"] = AC(inp["L0_attn_A_log"][i][cs])
            m[f"dpar{i}"] = AC(inp["L0_attn_D_par"][i][cs].reshape(-1, 1))
        maps.append(m)
    return maps


def kernel_mega(**inp):
    inp = {k: np.asarray(v) for k, v in inp.items()}
    maps = _mega_maps(inp)
    nc = get_prog("mega", build_mega)
    res = _run(nc, maps)
    out_fm = np.concatenate([np.asarray(res[c]["out"]) for c in range(NCORES)], 1)
    return np.ascontiguousarray(out_fm.T).reshape(B, S, D).astype(np.float32)


_kernel_multi = kernel


def kernel(**inp):
    try:
        return kernel_mega(**inp)
    except Exception as e:
        import traceback
        print("MEGA path failed, falling back to multi-launch:", e)
        traceback.print_exc(limit=5)
        return _kernel_multi(**inp)
